# version 19
# baseline (speedup 1.0000x reference)
import os
import numpy as np
import ml_dtypes
from contextlib import ExitStack

import concourse.bass as bass
import concourse.mybir as mybir
from concourse.bass_utils import run_bass_kernel_spmd

F32 = mybir.dt.float32
BF16 = mybir.dt.bfloat16
I32 = mybir.dt.int32
IOA = bass.IndirectOffsetOnAxis
AF = mybir.ActivationFunctionType
ALU = mybir.AluOpType
AX = mybir.AxisListType
bfnp = ml_dtypes.bfloat16

DM = 1024
DIN = 2048
DST = 128
NH = 32
HD = 64
CK = 64
DPROJ = 2 * DIN + 2 * DST + NH
NXBC = 18
NE = 8
DHID = 2048
CAPF = 1.25
EPS = 1e-5

ENGSEM = {"tensor": "pe", "vector": "dv", "scalar": "ac", "gpsimd": "gp"}
QSEM = {"sync": "qs", "scalar": "qa", "gpsimd": "qg"}
SKIP_SAME_ENG = os.environ.get("KQ_SIM_SAFE", "0") != "1"
SKIP_SEMS = set(os.environ.get("KQ_SKIP_SEMS", "").split(","))
PHASE_LIMIT = int(os.environ.get("KQ_PHASE", "9"))
NO_U = os.environ.get("KQ_NOU", "0") == "1"
NO_AG = os.environ.get("KQ_NOAG", "0") == "1"
NO_TR = os.environ.get("KQ_NOTR", "0") == "1"


class Res:
    __slots__ = ("w", "r", "name")

    def __init__(self, name=""):
        self.w = None
        self.r = {}
        self.name = name


class TB:
    __slots__ = ("t", "res")

    def __init__(self, t, name=""):
        self.t = t
        self.res = Res(name)

    def __getitem__(self, key):
        return self.t[key]

    def ap(self):
        return self.t[:]


def _res_of(x):
    return x.res if isinstance(x, TB) else x


class Bld:

    def __init__(self, nc, block):
        self.nc = nc
        self.block = block
        self.ops = {e: [] for e in ("sync", "tensor", "vector", "scalar", "gpsimd")}
        self.cnt = {s: 0 for s in ("pe", "dv", "ac", "gp", "cc")}
        self.S = {}
        self.waited = {}
        self.dsem = {}
        self.stack = None

    def setup(self, stack):
        self.stack = stack
        for s in list(self.cnt):
            self.S[s] = stack.enter_context(self.nc.semaphore(s))

    def _dma_sem(self, res):
        key = id(res)
        if key not in self.dsem:
            name = f"dq{len(self.dsem)}_{res.name[:12]}"
            self.S[name] = self.stack.enter_context(self.nc.semaphore(name))
            self.cnt[name] = 0
            self.dsem[key] = name
        return self.dsem[key]

    def _emit(self, eng, f):
        self.ops[eng].append(f)

    def _wait(self, eng, sem, val, allow_skip=False):
        if (allow_skip and SKIP_SAME_ENG and ENGSEM.get(eng) == sem
                and sem in SKIP_SEMS):
            return
        key = (eng, sem)
        if self.waited.get(key, 0) >= val:
            return
        self.waited[key] = val
        S = self.S[sem]
        self._emit(eng, lambda e: e.wait_ge(S, val))

    def _deps(self, eng, reads, writes, allow_skip=False):
        for r in reads:
            rr = _res_of(r)
            if rr.w:
                self._wait(eng, *rr.w, allow_skip=allow_skip)
        for w in writes:
            wr = _res_of(w)
            if wr.w:
                self._wait(eng, *wr.w, allow_skip=allow_skip)
            for sem, cv in wr.r.items():
                self._wait(eng, sem, cv, allow_skip=allow_skip)

    def _mark(self, reads, writes, sem, n):
        for w in writes:
            wr = _res_of(w)
            wr.w = (sem, n)
            wr.r = {}
        for r in reads:
            rr = _res_of(r)
            rr.r[sem] = max(rr.r.get(sem, 0), n)

    def op(self, eng, fn, reads=(), writes=()):
        self._deps(eng, reads, writes, allow_skip=True)
        sem = ENGSEM[eng]
        self.cnt[sem] += 1
        n = self.cnt[sem]
        S = self.S[sem]
        self._emit(eng, lambda e: fn(e).then_inc(S, 1))
        self._mark(reads, writes, sem, n)

    def mm(self, fns, reads=(), writes=()):
        if callable(fns):
            fns = [fns]
        self._deps("tensor", reads, writes, allow_skip=True)
        self.cnt["pe"] += 1
        n = self.cnt["pe"]
        S = self.S["pe"]

        def body(e):
            for i, fn in enumerate(fns):
                inst = fn(e)
                if i == len(fns) - 1:
                    inst.then_inc(S, 1)
        self._emit("tensor", body)
        self._mark(reads, writes, "pe", n)

    def dma(self, q, dst, src, reads=(), writes=(), transpose=False, slow=False):
        self._deps(q, reads, writes)
        assert writes, "dma needs a write Res for its semaphore"
        sem = self._dma_sem(_res_of(writes[0]))
        self.cnt[sem] += 16
        n = self.cnt[sem]
        S = self.S[sem]
        if transpose:
            self._emit(q, lambda e: e.dma_start(dst, src, transpose=True).then_inc(S, 16))
        elif slow:
            self._emit(q, lambda e: e.dma_start(
                dst, src, allow_slow_non_contiguous=True).then_inc(S, 16))
        else:
            self._emit(q, lambda e: e.dma_start(dst, src).then_inc(S, 16))
        self._mark(reads, writes, sem, n)

    def idma(self, dst, dst_off, src, src_off, reads=(), writes=(), bounds=None):
        self._deps("gpsimd", reads, writes)
        assert writes
        sem = self._dma_sem(_res_of(writes[0]))
        self.cnt[sem] += 16
        n = self.cnt[sem]
        S = self.S[sem]
        if bounds is None:
            self._emit("gpsimd", lambda e: e.indirect_dma_start(
                dst, dst_off, src, src_off).then_inc(S, 16))
        else:
            self._emit("gpsimd", lambda e: e.indirect_dma_start(
                dst, dst_off, src, src_off, bounds_check=bounds,
                oob_is_err=False).then_inc(S, 16))
        self._mark(reads, writes, sem, n)

    def cc(self, kind, alu, groups, in_ap, out_ap, reads=(), writes=()):
        self._deps("gpsimd", reads, writes)
        self.cnt["cc"] += 1
        n = self.cnt["cc"]
        S = self.S["cc"]
        self._emit("gpsimd", lambda e: e.collective_compute(
            kind, alu, replica_groups=groups, ins=[in_ap], outs=[out_ap],
        ).then_inc(S, 1))
        self._mark(reads, writes, "cc", n)

    def barrier(self):
        tgt = dict(self.cnt)
        for eng in ("sync", "tensor", "vector", "scalar", "gpsimd"):
            for sem, val in tgt.items():
                if val:
                    self._wait(eng, sem, val)

    def flush(self):
        for eng, lst in self.ops.items():
            def runner(e, lst=lst):
                for f in lst:
                    f(e)
            getattr(self.block, eng)(runner)


def hview(ap2d, P, reps):
    return ap2d.unsqueeze(2).broadcast_to([P, NH, reps])


def build_program(TLOC):
    assert TLOC % 128 == 0
    NCK = TLOC // CK
    NTT = TLOC // 128
    CAP = int(CAPF * (8 * TLOC) / NE)
    CAPP = 512 * ((CAP + 511) // 512)
    CAPQ = CAPP + 128
    NSLOTP = 2 * CAPQ
    SPPP = NSLOTP // 128
    NJK = CAPP // 128
    NQK = CAPP // 512
    blocks = []
    col = 0
    while col < TLOC:
        w = min(516, 4 + TLOC - col)
        blocks.append((col, w))
        col += w - 4
    NBLK = len(blocks)
    NQ = (TLOC + 511) // 512

    nc = bass.Bass("TRN2", target_bir_lowering=False)

    uT = nc.dram_tensor("uT", [DM, 4 + TLOC], F32, kind="ExternalInput").ap()
    u_tok = nc.dram_tensor("u_tok", [TLOC, DM], F32, kind="ExternalInput").ap()
    cconst = nc.dram_tensor("cconst", [64, 16], F32, kind="ExternalInput").ap()
    WinT_d = nc.dram_tensor("WinT", [DM, DPROJ], F32, kind="ExternalInput").ap()
    convw_d = nc.dram_tensor("convw", [128, NXBC, 4], F32, kind="ExternalInput").ap()
    convb_d = nc.dram_tensor("convb", [128, NXBC], F32, kind="ExternalInput").ap()
    dtb_d = nc.dram_tensor("dtb", [NH, 1], F32, kind="ExternalInput").ap()
    Alog_d = nc.dram_tensor("Alog", [NH, 1], F32, kind="ExternalInput").ap()
    Dhp_d = nc.dram_tensor("Dhp", [1, DIN], F32, kind="ExternalInput").ap()
    nrmw_d = nc.dram_tensor("nrmw", [128, 16], F32, kind="ExternalInput").ap()
    WoutT_d = nc.dram_tensor("WoutT", [DIN, DM], F32, kind="ExternalInput").ap()
    gateT_d = nc.dram_tensor("gateT", [DM, NE], F32, kind="ExternalInput").ap()
    gtok_d = nc.dram_tensor("gtok", [128, NTT], F32, kind="ExternalInput").ap()
    sidx_d = nc.dram_tensor("sidx", [128, 26], F32, kind="ExternalInput").ap()
    WgT_d = nc.dram_tensor("WgT", [NE, DM, DHID], F32, kind="ExternalInput").ap()
    WuT_d = nc.dram_tensor("WuT", [NE, DM, DHID], F32, kind="ExternalInput").ap()
    WdT_d = nc.dram_tensor("WdT", [NE, DHID, DM], F32, kind="ExternalInput").ap()
    out_d = nc.dram_tensor("out", [TLOC, DM], F32, kind="ExternalOutput").ap()

    sp_x = nc.dram_tensor("sp_x", [DIN, TLOC], BF16).ap()
    sp_z = nc.dram_tensor("sp_z", [DIN, TLOC], BF16).ap()
    sp_B = nc.dram_tensor("sp_B", [DST, TLOC], BF16).ap()
    sp_cs = nc.dram_tensor("sp_cs", [NH, TLOC], F32).ap()
    sp_pd = nc.dram_tensor("sp_pd", [NH, 9 * NBLK], F32).ap()
    sp_yn = nc.dram_tensor("sp_yn", [TLOC, DIN], BF16).ap()
    sp_xf = nc.dram_tensor("sp_xf", [TLOC, DM], BF16).ap()
    sp_wf = nc.dram_tensor("sp_wf", [8, NH], F32).ap()
    sp_rt = nc.dram_tensor("sp_rt", [1, 16 * NTT], F32).ap()
    sp_ro = nc.dram_tensor("sp_ro", [NTT, 16], F32).ap()
    ag_in = nc.dram_tensor("ag_in", [129, DIN], F32).ap()
    ag_out = nc.dram_tensor("ag_out", [8 * 129, DIN], F32, addr_space="Shared").ap()
    agc_in = nc.dram_tensor("agc_in", [1, 16], F32).ap()
    agc_out = nc.dram_tensor("agc_out", [8, 16], F32, addr_space="Shared").ap()
    tbl_d = nc.dram_tensor("tbl", [NE * NSLOTP, 2], F32).ap()
    tmn_d = nc.dram_tensor("tmn", [NSLOTP, 2], F32).ap()
    xe_d = nc.dram_tensor("xe", [2 * CAPP, DM], BF16).ap()
    agxf_out = nc.dram_tensor("agxf", [8 * TLOC, DM], BF16, addr_space="Shared").ap()
    agy_in = nc.dram_tensor("agy_in", [NSLOTP, DM], BF16).ap()
    agy_o = [nc.dram_tensor(f"agy_out{i}", [8 * CAPQ, DM], BF16,
                            addr_space="Shared").ap() for i in range(2)]

    c_ones128 = nc.inline_tensor(np.ones((128, 128), dtype=bfnp), name="c_ones128").ap()
    c_or64 = nc.inline_tensor(np.ones((1, 64), dtype=np.float32), name="c_or64").ap()
    c_or128 = nc.inline_tensor(np.ones((1, 128), dtype=np.float32), name="c_or128").ap()
    c_or128b = nc.inline_tensor(np.ones((1, 128), dtype=bfnp), name="c_or128b").ap()
    c_id64 = nc.inline_tensor(np.vstack([np.eye(64, dtype=bfnp)] * 2), name="c_id64").ap()
    c_id32 = nc.inline_tensor(np.eye(32, dtype=np.float32), name="c_id32").ap()
    c_triu64 = nc.inline_tensor(
        np.vstack([np.triu(np.ones((64, 64), dtype=np.float32))] * 2), name="c_triu64").ap()
    c_tri128 = nc.inline_tensor(
        np.triu(np.ones((128, 128), dtype=bfnp), k=1), name="c_tri128").ap()
    c_triNT = nc.inline_tensor(
        np.triu(np.ones((NTT, NTT), dtype=np.float32), k=1), name="c_triNT").ap()
    c_ocNT = nc.inline_tensor(np.ones((NTT, 1), dtype=np.float32), name="c_ocNT").ap()
    c_erow = nc.inline_tensor(
        np.tile((np.arange(NE) * CAPQ).astype(np.float32), (128, 1)),
        name="c_erow").ap()
    kb = np.zeros((128, 2), dtype=np.float32)
    kb[:, 0] = EPS
    kb[:, 1] = 1.0
    c_kbias = nc.inline_tensor(kb, name="c_kbias").ap()

    with ExitStack() as stack, nc.Block() as block:
        b = Bld(nc, block)
        b.setup(stack)

        pool = stack.enter_context(nc.psum_tensor("pool", [128, 4096], F32))
        BK = [Res(f"bk{i}") for i in range(8)]

        def pv(p0, pn, c0, cn):
            ap = pool[p0:p0 + pn, c0:c0 + cn]
            banks = tuple(BK[i] for i in range(c0 // 512, -(-(c0 + cn) // 512)))
            return ap, banks

        def sbt(st, name, shape, dtype, side=None):
            return TB(st.enter_context(nc.sbuf_tensor(name, shape, dtype, side=side)), name)

        cst = sbt(stack, "cst", [64, 16], F32, side="right")
        b.dma("sync", cst.ap(), cconst, writes=(cst,))
        dtb = sbt(stack, "dtb_s", [NH, 1], F32, side="right")
        b.dma("sync", dtb.ap(), dtb_d, writes=(dtb,))
        alog = sbt(stack, "alog_s", [NH, 1], F32, side="right")
        b.dma("sync", alog.ap(), Alog_d, writes=(alog,))
        convw = sbt(stack, "convw_s", [128, NXBC, 4], F32, side="right")
        b.dma("sync", convw.ap(), convw_d, writes=(convw,))
        convb = sbt(stack, "convb_s", [128, NXBC], F32, side="right")
        b.dma("sync", convb.ap(), convb_d, writes=(convb,))
        ones128 = sbt(stack, "ones128", [128, 128], BF16, side="right")
        b.dma("sync", ones128.ap(), c_ones128, writes=(ones128,))
        or64 = sbt(stack, "or64", [1, 64], F32, side="right")
        b.dma("sync", or64.ap(), c_or64, writes=(or64,))
        or128 = sbt(stack, "or128", [1, 128], F32, side="right")
        b.dma("sync", or128.ap(), c_or128, writes=(or128,))
        or128b = sbt(stack, "or128b", [1, 128], BF16, side="right")
        b.dma("sync", or128b.ap(), c_or128b, writes=(or128b,))
        id64 = sbt(stack, "id64", [128, 64], BF16, side="right")
        b.dma("sync", id64.ap(), c_id64, writes=(id64,))
        id32 = sbt(stack, "id32", [32, 32], F32, side="right")
        b.dma("sync", id32.ap(), c_id32, writes=(id32,))
        triu64 = sbt(stack, "triu64", [128, 64], F32, side="right")
        b.dma("sync", triu64.ap(), c_triu64, writes=(triu64,))
        tri128 = sbt(stack, "tri128", [128, 128], BF16, side="right")
        b.dma("sync", tri128.ap(), c_tri128, writes=(tri128,))
        triNT = sbt(stack, "triNT", [NTT, NTT], F32, side="right")
        b.dma("sync", triNT.ap(), c_triNT, writes=(triNT,))
        ocNT = sbt(stack, "ocNT", [NTT, 1], F32, side="right")
        b.dma("sync", ocNT.ap(), c_ocNT, writes=(ocNT,))
        kbias = sbt(stack, "kbias", [128, 2], F32, side="right")
        b.dma("sync", kbias.ap(), c_kbias, writes=(kbias,))

        A_col = sbt(stack, "A_col", [NH, 1], F32, side="right")
        b.op("scalar", lambda e: e.activation(A_col.ap(), alog.ap(), AF.Exp),
             reads=(alog,), writes=(A_col,))
        b.op("vector", lambda e: e.tensor_scalar_mul(A_col.ap(), A_col.ap(), -1.0),
             reads=(A_col,), writes=(A_col,))

        stP = ExitStack()
        dt_fm = sbt(stP, "dt_fm", [NH, TLOC], F32)
        cs_fm = sbt(stP, "cs_fm", [NH, TLOC], F32)
        B_fm = sbt(stP, "B_fm", [DST, TLOC], BF16)
        C_fm = sbt(stP, "C_fm", [DST, TLOC], BF16)
        csT = sbt(stP, "csT", [128, NTT, NH], F32)
        dtT = sbt(stP, "dtT", [128, NTT, NH], F32)

        spx_r = Res("sp_x")
        spz_r = Res("sp_z")
        spB_r = Res("sp_B")
        spcs_r = Res("sp_cs")
        sppd_r = Res("sp_pd")
        spyn_r = Res("sp_yn")
        spxf_r = Res("sp_xf")
        spwf_r = Res("sp_wf")
        tbl_r = Res("tbl")
        tmn_r = Res("tmn")
        agxf_r = Res("agxf")
        agyiR = [Res("agy_in0"), Res("agy_in1")]
        agyoR = [Res("agy_out0"), Res("agy_out1")]
        xeR = [Res("xe0"), Res("xe1")]
        sprt_r = Res("sp_rt")
        spro_r = Res("sp_ro")
        agi_r = Res("ag_in")
        ago_r = Res("ag_out")
        agci_r = Res("agc_in")
        agco_r = Res("agc_out")
        out_r = Res("out")

        DO_TILES = [(j, 128) for j in range(34)] + [(34, 32)]

        st1 = ExitStack()
        win = sbt(st1, "win", [128, 8, DPROJ], BF16)
        b.dma("gpsimd", win.ap(), WinT_d.rearrange("(k p) f -> p k f", p=128),
              writes=(win,))
        P_run = sbt(st1, "P_run", [NH, 1], F32)
        b.op("vector", lambda e: e.memset(P_run.ap(), 0.0), writes=(P_run,))
        pbe = sbt(st1, "pbe", [NH, 1], F32)
        That = sbt(st1, "That", [128, DIN], F32)
        ub = sbt(st1, "ub", [128, 8, 516], F32)
        m_sb = sbt(st1, "m_sb", [128, 516], F32)
        r_sb = sbt(st1, "r_sb", [128, 516], F32)
        xln = sbt(st1, "xln", [128, 8, 516], BF16)
        xsq = sbt(st1, "xsq", [128, 516], BF16)
        xb16 = sbt(st1, "xb16", [128, 516], BF16)
        xbc = [sbt(st1, f"xbc{i}", [128, 516], F32) for i in range(3)]
        cacc = sbt(st1, "cacc", [128, 512], F32)
        xstage = sbt(st1, "xstage", [128, 16, 512], BF16)
        zstage = sbt(st1, "zstage", [128, 16, 512], BF16)
        dtev = sbt(st1, "dtev", [NH, 512], F32)
        sp1 = sbt(st1, "sp1", [NH, 512], F32)
        sp2 = sbt(st1, "sp2", [NH, 512], F32)
        pd_blk = sbt(st1, "pd_blk", [NH, 9], F32)
        b.op("vector", lambda e: e.memset(pd_blk.ap(), 0.0), writes=(pd_blk,))
        pdflat = sbt(st1, "pdflat", [1, 9 * NH], F32)
        xT = [sbt(st1, f"xT{i}", [128, DIN], BF16) for i in range(2)]
        bT = [sbt(st1, f"bT{i}", [128, DST], BF16) for i in range(2)]
        scb = sbt(st1, "scb", [128, NH], BF16)
        scf = sbt(st1, "scf", [128, NH], F32)
        xds = sbt(st1, "xds", [128, DIN], BF16)
        resc = sbt(st1, "resc", [128, NH], F32)

        for kblk, (t0, W) in enumerate(blocks):
            WR = W - 4
            cpb = WR // CK
            b.dma("sync", ub[:, :, 0:W],
                  uT.rearrange("(k p) t -> p k t", p=128)[:, :, t0:t0 + W],
                  writes=(ub,))
            apA, bkA = pv(0, 128, 0, W)
            apB, bkB = pv(0, 128, 1024, W)
            apC, bkC = pv(0, 128, 2048, W)
            apD, bkD = pv(0, 128, 3072, W)
            wregs = [(apA, bkA), (apB, bkB), (apC, bkC), (apD, bkD)]
            for k in range(8):
                b.op("vector", lambda e, W=W, k=k: e.tensor_copy(xb16[:, 0:W], ub[:, k, 0:W]),
                     reads=(ub,), writes=(xb16,))
                for n0 in range(0, W, 512):
                    nw = min(512, W - n0)
                    b.mm(lambda e, apA=apA, k=k, n0=n0, nw=nw: e.matmul(
                        apA[:, n0:n0 + nw], ones128.ap(), xb16[:, n0:n0 + nw],
                        start=(k == 0), stop=(k == 7)),
                        reads=(xb16, ones128), writes=bkA)
                b.op("scalar", lambda e, W=W, k=k: e.activation(
                    xsq[:, 0:W], ub[:, k, 0:W], AF.Square),
                    reads=(ub,), writes=(xsq,))
                for n0 in range(0, W, 512):
                    nw = min(512, W - n0)
                    b.mm(lambda e, apB=apB, k=k, n0=n0, nw=nw: e.matmul(
                        apB[:, n0:n0 + nw], ones128.ap(), xsq[:, n0:n0 + nw],
                        start=(k == 0), stop=(k == 7)),
                        reads=(xsq, ones128), writes=bkB)
            b.op("vector", lambda e, W=W, apA=apA: e.tensor_scalar_mul(m_sb[:, 0:W], apA[:, 0:W], 1.0 / DM),
                 reads=bkA, writes=(m_sb,))
            b.op("vector", lambda e, W=W: e.scalar_tensor_tensor(
                r_sb[:, 0:W], m_sb[:, 0:W], -1.0, m_sb[:, 0:W], ALU.mult, ALU.mult),
                reads=(m_sb,), writes=(r_sb,))
            b.op("vector", lambda e, W=W, apB=apB: e.scalar_tensor_tensor(
                r_sb[:, 0:W], apB[:, 0:W], 1.0 / DM, r_sb[:, 0:W], ALU.mult, ALU.add),
                reads=bkB + (r_sb,), writes=(r_sb,))
            b.op("scalar", lambda e, W=W: e.activation(r_sb[:, 0:W], r_sb[:, 0:W], AF.Ln,
                                                  bias=kbias.t[:, 0:1]),
                 reads=(r_sb, kbias), writes=(r_sb,))
            b.op("scalar", lambda e, W=W: e.activation(r_sb[:, 0:W], r_sb[:, 0:W], AF.Exp, scale=-0.5),
                 reads=(r_sb,), writes=(r_sb,))
            for k in range(8):
                b.op("vector", lambda e, W=W, k=k: e.tensor_sub(
                    xln[:, k, 0:W], ub[:, k, 0:W], m_sb[:, 0:W]),
                    reads=(ub, m_sb), writes=(xln,))
                b.op("vector", lambda e, W=W, k=k: e.tensor_mul(
                    xln[:, k, 0:W], xln[:, k, 0:W], r_sb[:, 0:W]),
                    reads=(xln, r_sb), writes=(xln,))

            for jt, nr in DO_TILES:
                app, bks = wregs[jt % 4]
                fns = []
                for k in range(8):
                    for n0 in range(0, W, 512):
                        nw = min(512, W - n0)
                        fns.append(lambda e, jt=jt, k=k, n0=n0, nw=nw, nr=nr, app=app:
                                   e.matmul(app[0:nr, n0:n0 + nw],
                                            win[:, k, jt * 128: jt * 128 + nr],
                                            xln[:, k, n0:n0 + nw],
                                            start=(k == 0), stop=(k == 7)))
                b.mm(fns, reads=(win, xln), writes=bks)
                if jt < 16:
                    b.op("scalar", lambda e, W=W, WR=WR, jt=jt, app=app: e.activation(
                        zstage[:, jt, 0:WR], app[:, 4:W], AF.Silu),
                        reads=bks, writes=(zstage,))
                elif jt < 34:
                    jx = jt - 16
                    xb = xbc[jx % 3]
                    b.op("scalar", lambda e, W=W, app=app, xb=xb: e.copy(
                        xb[:, 0:W], app[:, 0:W]), reads=bks, writes=(xb,))
                    b.op("vector", lambda e, WR=WR, jx=jx, xb=xb: e.tensor_scalar(
                        cacc[:, 0:WR], xb[:, 1:1 + WR], convw.t[:, jx, 0:1],
                        convb.t[:, jx:jx + 1], ALU.mult, ALU.add),
                        reads=(xb, convw, convb), writes=(cacc,))
                    for kk in range(1, 4):
                        b.op("vector", lambda e, WR=WR, jx=jx, kk=kk, xb=xb: e.scalar_tensor_tensor(
                            cacc[:, 0:WR], xb[:, 1 + kk:1 + kk + WR], convw.t[:, jx, kk:kk + 1],
                            cacc[:, 0:WR], ALU.mult, ALU.add),
                            reads=(xb, convw, cacc), writes=(cacc,))
                    if jx < 16:
                        b.op("scalar", lambda e, WR=WR, jx=jx: e.activation(
                            xstage[:, jx, 0:WR], cacc[:, 0:WR], AF.Silu),
                            reads=(cacc,), writes=(xstage,))
                    elif jx == 16:
                        b.op("scalar", lambda e, WR=WR, t0=t0: e.activation(
                            B_fm.t[:, t0:t0 + WR], cacc[:, 0:WR], AF.Silu),
                            reads=(cacc,), writes=(B_fm,))
                    else:
                        b.op("scalar", lambda e, WR=WR, t0=t0: e.activation(
                            C_fm.t[:, t0:t0 + WR], cacc[:, 0:WR], AF.Silu),
                            reads=(cacc,), writes=(C_fm,))
                else:
                    b.op("vector", lambda e, W=W, WR=WR, app=app: e.tensor_scalar_add(
                        dtev[:, 0:WR], app[0:NH, 4:W], dtb.t[:, 0:1]),
                        reads=bks + (dtb,), writes=(dtev,))

            b.op("scalar", lambda e, WR=WR: e.activation(sp1[:, 0:WR], dtev[:, 0:WR], AF.Abs),
                 reads=(dtev,), writes=(sp1,))
            b.op("scalar", lambda e, WR=WR: e.activation(sp1[:, 0:WR], sp1[:, 0:WR], AF.Exp, scale=-1.0),
                 reads=(sp1,), writes=(sp1,))
            b.op("scalar", lambda e, WR=WR: e.activation(sp1[:, 0:WR], sp1[:, 0:WR], AF.Ln,
                                                  bias=kbias.t[0:NH, 1:2]),
                 reads=(sp1, kbias), writes=(sp1,))
            b.op("vector", lambda e, WR=WR: e.tensor_scalar_max(sp2[:, 0:WR], dtev[:, 0:WR], 0.0),
                 reads=(dtev,), writes=(sp2,))
            b.op("vector", lambda e, WR=WR, t0=t0: e.tensor_add(
                dt_fm.t[:, t0:t0 + WR], sp1[:, 0:WR], sp2[:, 0:WR]),
                reads=(sp1, sp2), writes=(dt_fm,))

            b.dma("scalar", sp_x.rearrange("(j p) t -> p j t", p=128)[:, :, t0:t0 + WR],
                  xstage[:, :, 0:WR], reads=(xstage,), writes=(spx_r,))
            b.dma("scalar", sp_z.rearrange("(j p) t -> p j t", p=128)[:, :, t0:t0 + WR],
                  zstage[:, :, 0:WR], reads=(zstage,), writes=(spz_r,))
            b.dma("scalar", sp_B[:, t0:t0 + WR], B_fm.t[:, t0:t0 + WR],
                  reads=(B_fm,), writes=(spB_r,))

            b.op("vector", lambda e, WR=WR, t0=t0: e.tensor_scalar_mul(
                dtev[:, 0:WR], dt_fm.t[:, t0:t0 + WR], A_col.t[:, 0:1]),
                reads=(dt_fm, A_col), writes=(dtev,))
            b.op("vector", lambda e: e.memset(sp2[:, 0:CK], 0.0), writes=(sp2,))
            for c in range(cpb):
                b.op("vector", lambda e, t0=t0, c=c: e.tensor_tensor_scan(
                    cs_fm.t[:, t0 + c * CK: t0 + (c + 1) * CK],
                    dtev[:, c * CK:(c + 1) * CK], sp2[:, 0:CK], 0.0,
                    ALU.add, ALU.add),
                    reads=(dtev, sp2), writes=(cs_fm,))
            avw = bass.AP(cs_fm.t, t0 + CK - 1, [[TLOC, NH], [CK, cpb]])
            b.op("vector", lambda e, avw=avw, cpb=cpb: e.tensor_tensor_scan(
                pd_blk[:, 0:cpb], avw, sp2[:, 0:cpb], P_run.t[:, 0:1],
                ALU.add, ALU.add),
                reads=(cs_fm, sp2, P_run), writes=(pd_blk,))
            b.op("vector", lambda e, cpb=cpb: e.tensor_copy(pbe.ap(), pd_blk[:, cpb - 1:cpb]),
                 reads=(pd_blk,), writes=(pbe,))
            b.op("vector", lambda e: e.tensor_sub(pd_blk[:, 8:9], P_run.ap(), pbe.ap()),
                 reads=(P_run, pbe), writes=(pd_blk,))
            b.op("vector", lambda e: e.tensor_copy(P_run.ap(), pbe.ap()),
                 reads=(pbe,), writes=(P_run,))
            b.op("vector", lambda e, avw=avw, cpb=cpb: e.tensor_sub(pd_blk[:, 0:cpb], pd_blk[:, 0:cpb], avw),
                 reads=(pd_blk, cs_fm), writes=(pd_blk,))
            b.op("vector", lambda e, cpb=cpb: e.tensor_scalar(
                pd_blk[:, 0:cpb], pd_blk[:, 0:cpb], pbe.t[:, 0:1], None, ALU.subtract),
                reads=(pd_blk, pbe), writes=(pd_blk,))

            b.dma("scalar", sp_cs[:, t0:t0 + WR], cs_fm.t[:, t0:t0 + WR],
                  reads=(cs_fm,), writes=(spcs_r,))
            b.dma("scalar", sp_pd[:, 9 * kblk: 9 * (kblk + 1)], pd_blk.ap(),
                  reads=(pd_blk,), writes=(sppd_r,))
            b.dma("scalar", pdflat.ap().rearrange("o (h c) -> o h c", h=NH),
                  bass.AP(sp_pd.tensor, 9 * kblk, [[1, 1], [9 * NBLK, NH], [1, 9]]),
                  reads=(sppd_r,), writes=(pdflat,))

            tt0 = t0 // 128
            apT, bkT = pv(0, 128, 2048, 64)
            for tt in range(tt0, tt0 + (0 if NO_TR else WR // 128)):
                b.mm(lambda e, apT=apT, tt=tt: e.transpose(
                    apT[:, 0:NH], cs_fm.t[:, tt * 128:(tt + 1) * 128], id32.ap()),
                    reads=(cs_fm, id32), writes=bkT)
                b.op("vector", lambda e, apT=apT, tt=tt: e.tensor_copy(
                    csT.t[:, tt, :], apT[:, 0:NH]), reads=bkT, writes=(csT,))
                b.mm(lambda e, apT=apT, tt=tt: e.transpose(
                    apT[:, 0:NH], dt_fm.t[:, tt * 128:(tt + 1) * 128], id32.ap()),
                    reads=(dt_fm, id32), writes=bkT)
                b.op("vector", lambda e, apT=apT, tt=tt: e.tensor_copy(
                    dtT.t[:, tt, :], apT[:, 0:NH]), reads=bkT, writes=(dtT,))

            apU, bkU = pv(0, 128, 0, 2048)
            apU2, bkU2 = pv(0, 128, 2048, 2048)
            apb_, bkb_ = pv(0, 128, 2048 + 1024, 64)
            for pi in range(0 if NO_U else cpb // 2):
                tt = (t0 // 128) + pi
                xTt = xT[pi % 2]
                bTt = bT[pi % 2]
                b.dma("sync", xTt.ap(), sp_x[:, tt * 128:(tt + 1) * 128],
                      reads=(spx_r,), writes=(xTt,), transpose=True)
                b.dma("sync", bTt.ap(), sp_B[:, tt * 128:(tt + 1) * 128],
                      reads=(spB_r,), writes=(bTt,), transpose=True)
                b.mm([lambda e, apb_=apb_, ci=2 * pi + hh: e.matmul(
                    apb_[64 * (ci % 2):64 * (ci % 2) + 64, 0:NH], or64.ap(),
                    bass.AP(pdflat.t, ci, [[9 * NH, 1], [9, NH]]),
                    start=True, stop=True) for hh in range(2)],
                    reads=(pdflat, or64), writes=bkb_)
                b.op("vector", lambda e, apb_=apb_, tt=tt: e.tensor_add(
                    scf.ap(), csT.t[:, tt, :], apb_[:, 0:NH]),
                    reads=(csT,) + bkb_, writes=(scf,))
                b.op("scalar", lambda e: e.activation(scf.ap(), scf.ap(), AF.Exp, scale=-1.0),
                     reads=(scf,), writes=(scf,))
                b.op("vector", lambda e, tt=tt: e.tensor_mul(
                    scb.ap(), scf.ap(), dtT.t[:, tt, :]),
                    reads=(scf, dtT), writes=(scb,))
                scv = hview(scb[:, :], 128, HD)
                b.op("vector", lambda e, scv=scv, xTt=xTt: e.tensor_tensor(
                    xds.ap().rearrange("p (h d) -> p h d", h=NH),
                    xTt.ap().rearrange("p (h d) -> p h d", h=NH),
                    scv, ALU.mult),
                    reads=(xTt, scb), writes=(xds,))
                fns = []
                for hh in range(2):
                    ci = 2 * pi + hh
                    apx = apU if hh == 0 else apU2
                    for n0 in range(0, DIN, 512):
                        fns.append(lambda e, apx=apx, n0=n0, ci=ci, hh=hh, bTt=bTt, cpb=cpb:
                                   e.matmul(apx[:, n0:n0 + 512],
                                            bTt[64 * hh:64 * hh + 64, :],
                                            xds.t[64 * hh:64 * hh + 64, n0:n0 + 512],
                                            start=(ci < 2), stop=(ci >= cpb - 2)))
                b.mm(fns, reads=(bTt, xds), writes=bkU + bkU2)

            apb2, bkb2 = pv(0, 128, 1024, 64)
            if NO_U:
                pass
            elif kblk == 0:
                b.op("vector", lambda e, apU=apU: e.tensor_copy(That.ap(), apU[:, 0:DIN]),
                     reads=bkU, writes=(That,))
            else:
                b.mm(lambda e, apb2=apb2: e.matmul(
                    apb2[:, 0:NH], or128.ap(),
                    bass.AP(pdflat.t, 8, [[9 * NH, 1], [9, NH]]),
                    start=True, stop=True),
                    reads=(pdflat, or128), writes=bkb2)
                b.op("scalar", lambda e, apb2=apb2: e.activation(
                    resc.ap(), apb2[:, 0:NH], AF.Exp, scale=-1.0),
                    reads=bkb2, writes=(resc,))
                rv = hview(resc[:, :], 128, HD)
                b.op("vector", lambda e, rv=rv: e.tensor_tensor(
                    That.ap().rearrange("p (h d) -> p h d", h=NH),
                    That.ap().rearrange("p (h d) -> p h d", h=NH), rv, ALU.mult),
                    reads=(That, resc), writes=(That,))
                b.op("vector", lambda e, apU=apU: e.tensor_add(That.ap(), That.ap(), apU[:, 0:DIN]),
                     reads=(That,) + bkU, writes=(That,))

        b.dma("scalar", ag_in[0:128, :], That.ap(), reads=(That,), writes=(agi_r,))
        b.dma("scalar", ag_in[128:129, 0:NH], P_run.ap(), reads=(P_run,), writes=(agi_r,))
        b.op("vector", lambda e: e.memset(resc.ap(), 0.0), writes=(resc,))
        b.dma("scalar", ag_in[128:129, NH:DIN],
              bass.AP(resc.t, 0, [[NH, 1], [0, (DIN - NH) // NH], [1, NH]]),
              reads=(resc,), writes=(agi_r,))
        if not NO_AG:
            b.cc("AllGather", ALU.bypass, [[0, 1, 2, 3, 4, 5, 6, 7]],
                 ag_in, ag_out, reads=(agi_r,), writes=(ago_r,))
        b.barrier()
        st1.close()
        x1 = sbt(stack, "x1", [128, NTT, DM], F32, side="right")
        gtb = sbt(stack, "gtb", [128, NTT], F32, side="right")
        b.dma("sync", gtb.ap(), gtok_d, writes=(gtb,))
        erow = sbt(stack, "erow", [128, NE], F32, side="right")
        b.dma("sync", erow.ap(), c_erow, writes=(erow,))
        sidxw = sbt(stack, "sidxw", [128, 26], F32, side="right")
        b.dma("sync", sidxw.ap(), sidx_d, writes=(sidxw,))
        sidxi = sbt(stack, "sidxi", [128, 26], I32, side="right")
        b.op("vector", lambda e: e.tensor_copy(sidxi.ap(), sidxw.ap()),
             reads=(sidxw,), writes=(sidxi,))
        idxg = sbt(stack, "idxg", [128, NTT, 2], I32, side="right")

        if PHASE_LIMIT < 2:
            b.dma("sync", out_d.rearrange("(a p) f -> p a f", p=128), x1.ap(),
                  reads=(x1,), writes=(out_r,))
            b.barrier()
            b.flush()
            return nc
        st2 = ExitStack()
        st2a = ExitStack()
        T = sbt(st2, "T", [128, DIN], F32)
        atot = sbt(st2a, "atot", [8, NH], F32)
        b.dma("sync", atot.ap(),
              bass.AP(ag_out.tensor, 128 * DIN, [[129 * DIN, 8], [1, NH]]),
              reads=(ago_r,), writes=(atot,))
        apw, bkw = pv(0, 8, 0, 32)
        b.mm(lambda e: e.matmul(apw[:, 0:NH], cst.t[0:8, 2:10], atot.ap(),
                                start=True, stop=True),
             reads=(cst, atot), writes=bkw)
        wall = sbt(st2a, "wall", [8, NH], F32)
        b.op("scalar", lambda e: e.activation(wall.ap(), apw[:, 0:NH], AF.Exp),
             reads=bkw, writes=(wall,))
        b.op("vector", lambda e: e.tensor_scalar_mul(wall.ap(), wall.ap(), cst.t[0:8, 0:1]),
             reads=(wall, cst), writes=(wall,))
        b.dma("scalar", sp_wf[:, :], wall.ap(), reads=(wall,), writes=(spwf_r,))
        wfl = sbt(st2a, "wfl", [1, 8 * NH], F32)
        b.dma("scalar", wfl.ap(), sp_wf.rearrange("(o j) h -> o (j h)", o=1),
              reads=(spwf_r,), writes=(wfl,))
        apwb, bkwb = pv(0, 128, 0, 256)
        b.mm(lambda e: e.matmul(apwb[:, 0:256], or128.ap(), wfl.ap(),
                                start=True, stop=True),
             reads=(or128, wfl), writes=bkwb)
        wb = sbt(st2a, "wb", [128, 8 * NH], F32)
        b.op("scalar", lambda e: e.copy(wb.ap(), apwb[:, 0:256]),
             reads=bkwb, writes=(wb,))

        Sj = [sbt(st2a, "Sj0", [128, DIN], F32)] * 2
        apTmp, bkTmp = pv(0, 128, 0, 2048)
        for j in range(8):
            Sjt = Sj[j % 2]
            b.dma("sync", Sjt.ap(), ag_out[j * 129:j * 129 + 128, :],
                  reads=(ago_r,), writes=(Sjt,))
            wv = hview(wb[:, j * NH:(j + 1) * NH], 128, HD)
            if j == 0:
                b.op("vector", lambda e, Sjt=Sjt, wv=wv: e.tensor_tensor(
                    T.ap().rearrange("p (h d) -> p h d", h=NH),
                    Sjt.ap().rearrange("p (h d) -> p h d", h=NH), wv, ALU.mult),
                    reads=(Sjt, wb), writes=(T,))
            else:
                b.op("vector", lambda e, Sjt=Sjt, wv=wv: e.tensor_tensor(
                    apTmp[:, 0:DIN].rearrange("p (h d) -> p h d", h=NH),
                    Sjt.ap().rearrange("p (h d) -> p h d", h=NH), wv, ALU.mult),
                    reads=(Sjt, wb), writes=bkTmp)
                b.op("vector", lambda e: e.tensor_add(T.ap(), T.ap(), apTmp[:, 0:DIN]),
                     reads=(T,) + bkTmp, writes=(T,))

        b.barrier()
        st2a.close()
        ecs = sbt(st2, "ecs", [128, NTT, NH], F32)
        b.op("scalar", lambda e: e.activation(
            ecs.ap().rearrange("p a h -> p (a h)"),
            csT.ap().rearrange("p a h -> p (a h)"), AF.Exp),
            reads=(csT,), writes=(ecs,))

        dbr = sbt(st2, "dbr", [1, DIN], BF16)
        b.dma("gpsimd", dbr.ap(), Dhp_d, writes=(dbr,))
        apD, bkD = pv(0, 128, 0, 2048)
        b.mm([lambda e, n0=n0: e.matmul(apD[:, n0:n0 + 512], or128b.ap(),
                                        dbr.t[:, n0:n0 + 512], start=True, stop=True)
              for n0 in range(0, DIN, 512)],
             reads=(or128b, dbr), writes=bkD)
        Db = sbt(st2, "Db", [128, DIN], BF16)
        b.op("scalar", lambda e: e.copy(Db.ap(), apD[:, 0:DIN]), reads=bkD, writes=(Db,))

        TzB = [sbt(st2, f"TzB{i}", [128, DIN], BF16) for i in range(2)]
        zT = [sbt(st2, f"zT{i}", [128, DIN], BF16) for i in range(2)]
        xT2 = [sbt(st2, f"xT2{i}", [128, DIN], BF16) for i in range(2)]
        bT2 = [sbt(st2, f"bT2{i}", [128, DST], BF16) for i in range(2)]
        xdt = [sbt(st2, f"xdt{i}", [128, DIN], BF16) for i in range(2)]
        xDt = [sbt(st2, "xDt0", [128, DIN], BF16)] * 2
        xds2 = [sbt(st2, f"xds2{i}", [128, DIN], BF16) for i in range(2)]
        csflc = sbt(st2, "csflc", [1, 2 * NH * CK], F32)
        cstl2 = [sbt(st2, f"cstl{i}", [1, 2 * NH], F32) for i in range(2)]
        dzb2 = sbt(st2, "dzb2", [128, 2 * NH], F32)
        scf2 = sbt(st2, "scf2", [128, NH], F32)
        segE = sbt(st2, "segE", [128, DIN], F32)
        MT = [sbt(st2, "MT0", [128, DIN], BF16)] * 2
        GMT = sbt(st2, "GMT", [128, 64], BF16)
        ygf = sbt(st2, "ygf", [128, DIN], F32)
        ynb = [sbt(st2, f"ynb{i}", [128, DIN], BF16) for i in range(2)]
        rss = sbt(st2, "rss", [128, 1], F32)
        rr_ = sbt(st2, "rr_", [128, 1], F32)
        dz01 = sbt(st2, "dz01", [128, NH], F32)
        Wsc = sbt(st2, "Wsc", [128, DIN], BF16)

        for tt in range(NTT):
            par = tt % 2
            cstl = cstl2[par]
            xTt = xT2[par]
            zTt = zT[par]
            bTt = bT2[par]
            xdtt = xdt[par]
            xDtt = xDt[par]
            xdst = xds2[par]
            MTt = MT[par]
            ynbt = ynb[par]
            zA = tt * 2
            b.dma("sync", xTt.ap(), sp_x[:, tt * 128:(tt + 1) * 128],
                  reads=(spx_r,), writes=(xTt,), transpose=True)
            b.dma("sync", zTt.ap(), sp_z[:, tt * 128:(tt + 1) * 128],
                  reads=(spz_r,), writes=(zTt,), transpose=True)
            b.dma("sync", bTt.ap(), sp_B[:, tt * 128:(tt + 1) * 128],
                  reads=(spB_r,), writes=(bTt,), transpose=True)
            b.dma("scalar", csflc.ap().rearrange("o (c h l) -> o c h l", c=2, h=NH),
                  bass.AP(sp_cs.tensor, zA * CK,
                          [[1, 1], [CK, 2], [TLOC, NH], [1, CK]]),
                  reads=(spcs_r,), writes=(csflc,))
            b.dma("scalar", cstl.ap().rearrange("o (c h) -> o c h", c=2),
                  bass.AP(sp_cs.tensor, zA * CK + CK - 1,
                          [[1, 1], [CK, 2], [TLOC, NH]]),
                  reads=(spcs_r,), writes=(cstl,), slow=True)
            dtv = hview(dtT[:, tt, :], 128, HD)
            b.op("vector", lambda e, xTt=xTt, xdtt=xdtt, dtv=dtv: e.tensor_tensor(
                xdtt.ap().rearrange("p (h d) -> p h d", h=NH),
                xTt.ap().rearrange("p (h d) -> p h d", h=NH), dtv, ALU.mult),
                reads=(xTt, dtT), writes=(xdtt,))
            b.op("vector", lambda e, xTt=xTt, xDtt=xDtt: e.tensor_tensor(
                xDtt.ap(), xTt.ap(), Db.ap(), ALU.mult),
                reads=(xTt, Db), writes=(xDtt,))
            apd, bkd = pv(0, 128, 0, 64)
            b.mm([lambda e, apd=apd, hh=hh, cstl=cstl: e.matmul(
                apd[:, 32 * hh:32 * hh + NH], or128.ap(),
                bass.AP(cstl.t, hh * NH, [[2 * NH, 1], [1, NH]]),
                start=True, stop=True) for hh in range(2)],
                reads=(or128, cstl), writes=bkd)
            b.op("scalar", lambda e, apd=apd: e.activation(
                dzb2.ap(), apd[:, 0:2 * NH], AF.Exp), reads=bkd, writes=(dzb2,))
            apa, bka = pv(0, 128, 64, 32)
            b.mm([lambda e, apa=apa, hh=hh, cstl=cstl: e.matmul(
                apa[64 * hh:64 * hh + 64, 0:NH], or64.ap(),
                bass.AP(cstl.t, hh * NH, [[2 * NH, 1], [1, NH]]),
                start=True, stop=True) for hh in range(2)],
                reads=(or64, cstl), writes=bka)
            b.op("vector", lambda e, apa=apa, tt=tt: e.tensor_sub(
                scf2.ap(), csT.t[:, tt, :], apa[:, 0:NH]),
                reads=(csT,) + bka, writes=(scf2,))
            b.op("scalar", lambda e: e.activation(
                scf2.ap(), scf2.ap(), AF.Exp, scale=-1.0),
                reads=(scf2,), writes=(scf2,))
            scv = hview(scf2[:, :], 128, HD)
            b.op("vector", lambda e, scv=scv, xdtt=xdtt, xdst=xdst: e.tensor_tensor(
                xdst.ap().rearrange("p (h d) -> p h d", h=NH),
                xdtt.ap().rearrange("p (h d) -> p h d", h=NH),
                scv, ALU.mult),
                reads=(xdtt, scf2), writes=(xdst,))
            apUU, bkUU = pv(0, 128, 2048, 2048)
            dzv0 = dzb2.t[:, 0:NH].unsqueeze(2).broadcast_to([128, NH, HD])
            dzv1 = dzb2.t[:, NH:2 * NH].unsqueeze(2).broadcast_to([128, NH, HD])
            dz01v = dz01.t[:, :].unsqueeze(2).broadcast_to([128, NH, HD])
            b.mm([lambda e, apUU=apUU, n0=n0, bTt=bTt, xdst=xdst: e.matmul(
                apUU[:, n0:n0 + 512], bTt[0:64, :],
                xdst.t[0:64, n0:n0 + 512],
                start=True, stop=True)
                for n0 in range(0, DIN, 512)],
                reads=(bTt, xdst), writes=bkUU)
            b.op("vector", lambda e: e.tensor_mul(
                dz01.ap(), dzb2.t[:, 0:NH], dzb2.t[:, NH:2 * NH]),
                reads=(dzb2,), writes=(dz01,))
            b.op("scalar", lambda e: e.copy(TzB[0].ap(), T.ap()),
                 reads=(T,), writes=(TzB[0],))
            b.op("vector", lambda e, dzv0=dzv0: e.tensor_tensor(
                TzB[1].ap().rearrange("p (h d) -> p h d", h=NH),
                T.ap().rearrange("p (h d) -> p h d", h=NH), dzv0, ALU.mult),
                reads=(T, dzb2), writes=(TzB[1],))
            b.op("vector", lambda e, apUU=apUU: e.tensor_add(
                TzB[1].ap(), TzB[1].ap(), apUU[:, 0:DIN]),
                reads=(TzB[1],) + bkUU, writes=(TzB[1],))
            b.op("vector", lambda e, apUU=apUU, dzv1=dzv1: e.tensor_tensor(
                Wsc.ap().rearrange("p (h d) -> p h d", h=NH),
                apUU[:, 0:DIN].rearrange("p (h d) -> p h d", h=NH),
                dzv1, ALU.mult),
                reads=(dzb2,) + bkUU, writes=(Wsc,))
            b.mm([lambda e, apUU=apUU, n0=n0, bTt=bTt, xdst=xdst: e.matmul(
                apUU[:, n0:n0 + 512], bTt[64:128, :],
                xdst.t[64:128, n0:n0 + 512],
                start=True, stop=True)
                for n0 in range(0, DIN, 512)],
                reads=(bTt, xdst), writes=bkUU)
            b.op("vector", lambda e, dz01v=dz01v: e.tensor_tensor(
                T.ap().rearrange("p (h d) -> p h d", h=NH),
                T.ap().rearrange("p (h d) -> p h d", h=NH), dz01v, ALU.mult),
                reads=(T, dz01), writes=(T,))
            b.op("vector", lambda e: e.tensor_add(T.ap(), T.ap(), Wsc.ap()),
                 reads=(T, Wsc), writes=(T,))
            b.op("vector", lambda e, apUU=apUU: e.tensor_add(
                T.ap(), T.ap(), apUU[:, 0:DIN]),
                reads=(T,) + bkUU, writes=(T,))
            apTB, bkTB = pv(0, 128, 0, 2048)
            fns = []
            for hh in range(2):
                for h0 in (0, 8, 16, 24):
                    fns.append(lambda e, apTB=apTB, hh=hh, h0=h0, csflc=csflc: e.matmul(
                        apTB[64 * hh:64 * hh + 64, h0 * CK: h0 * CK + 512],
                        or64.ap(),
                        bass.AP(csflc.t, hh * NH * CK + h0 * CK,
                                [[2 * NH * CK, 1], [CK, 8], [1, CK]]),
                        start=True, stop=True))
            b.mm(fns, reads=(or64, csflc), writes=bkTB)
            csv = csT.t[:, tt, :].unsqueeze(2).broadcast_to([128, NH, CK])
            b.op("vector", lambda e, apTB=apTB, csv=csv: e.tensor_tensor(
                segE.ap().rearrange("p (h l) -> p h l", h=NH),
                apTB[:, 0:2048].rearrange("p (h l) -> p h l", h=NH),
                csv, ALU.subtract),
                reads=(csT,) + bkTB, writes=(segE,))
            b.op("vector", lambda e: e.tensor_single_scalar(
                segE.ap(), segE.ap(), 0.0, ALU.min),
                reads=(segE,), writes=(segE,))
            b.op("scalar", lambda e: e.activation(segE.ap(), segE.ap(), AF.Exp),
                 reads=(segE,), writes=(segE,))
            apG, bkG = pv(0, 128, 0, 64)
            b.mm([lambda e, apG=apG, hh=hh, zz=zA + hh: e.matmul(
                apG[64 * hh:64 * hh + 64, 0:64], B_fm.t[:, zz * CK:(zz + 1) * CK],
                C_fm.t[:, zz * CK:(zz + 1) * CK], start=True, stop=True)
                for hh in range(2)],
                reads=(B_fm, C_fm), writes=bkG)
            b.op("vector", lambda e, apG=apG: e.tensor_tensor(
                GMT.ap(), apG[:, 0:64], triu64.ap(), ALU.mult),
                reads=bkG + (triu64,), writes=(GMT,))
            gmv = GMT.t[:, :].unsqueeze(1).broadcast_to([128, NH, CK])
            b.op("vector", lambda e, gmv=gmv, MTt=MTt: e.tensor_tensor(
                MTt.ap().rearrange("p (h l) -> p h l", h=NH),
                segE.ap().rearrange("p (h l) -> p h l", h=NH),
                gmv, ALU.mult),
                reads=(segE, GMT), writes=(MTt,))
            apYB, bkYB = pv(0, 128, 2048, 2048)
            fns = []
            for hh in range(2):
                for n0 in range(0, DIN, 512):
                    fns.append(lambda e, apYB=apYB, hh=hh, n0=n0, xDtt=xDtt: e.matmul(
                        apYB[64 * hh:64 * hh + 64, n0:n0 + 512],
                        id64.t[64 * hh:64 * hh + 64, :],
                        xDtt[64 * hh:64 * hh + 64, n0:n0 + 512],
                        start=True, stop=False))
                for h in range(NH):
                    fns.append(lambda e, apYB=apYB, hh=hh, h=h, MTt=MTt, xdtt=xdtt: e.matmul(
                        apYB[64 * hh:64 * hh + 64, h * HD:(h + 1) * HD],
                        MTt.t[64 * hh:64 * hh + 64, h * HD:(h + 1) * HD],
                        xdtt.t[64 * hh:64 * hh + 64, h * HD:(h + 1) * HD],
                        start=False, stop=(h % 8 == 7)))
            b.mm(fns, reads=(id64, xDtt, MTt, xdtt), writes=bkYB)
            apYA, bkYA = pv(0, 128, 0, 2048)
            fns = []
            for hh in range(2):
                for n0 in range(0, DIN, 512):
                    fns.append(lambda e, apYA=apYA, hh=hh, n0=n0, zz=zA + hh: e.matmul(
                        apYA[64 * hh:64 * hh + 64, n0:n0 + 512],
                        C_fm.t[:, zz * CK:(zz + 1) * CK],
                        TzB[hh][:, n0:n0 + 512],
                        start=True, stop=True))
            b.mm(fns, reads=(C_fm, TzB[0], TzB[1]), writes=bkYA)
            ecv = ecs.t[:, tt, :].unsqueeze(2).broadcast_to([128, NH, HD])
            b.op("vector", lambda e, apYA=apYA, ecv=ecv: e.tensor_tensor(
                ygf.ap().rearrange("p (h d) -> p h d", h=NH),
                apYA[:, 0:2048].rearrange("p (h d) -> p h d", h=NH),
                ecv, ALU.mult),
                reads=(ecs,) + bkYA, writes=(ygf,))
            b.op("vector", lambda e, apYB=apYB: e.tensor_add(
                ygf.ap(), ygf.ap(), apYB[:, 0:2048]),
                reads=(ygf,) + bkYB, writes=(ygf,))
            b.op("vector", lambda e, zTt=zTt: e.tensor_mul(
                ygf.ap(), ygf.ap(), zTt.ap()),
                reads=(ygf, zTt), writes=(ygf,))
            b.op("scalar", lambda e, xdst=xdst: e.activation(
                xdst.ap(), ygf.ap(), AF.Square, accum_out=rss.ap()),
                reads=(ygf,), writes=(xdst, rss))
            b.op("vector", lambda e: e.tensor_scalar(
                rr_.ap(), rss.ap(), 1.0 / DIN, EPS, ALU.mult, ALU.add),
                reads=(rss,), writes=(rr_,))
            b.op("scalar", lambda e: e.activation(rr_.ap(), rr_.ap(), AF.Ln),
                 reads=(rr_,), writes=(rr_,))
            b.op("scalar", lambda e: e.activation(rr_.ap(), rr_.ap(), AF.Exp, scale=-0.5),
                 reads=(rr_,), writes=(rr_,))
            b.op("vector", lambda e, ynbt=ynbt: e.tensor_scalar_mul(
                ynbt.ap(), ygf.ap(), rr_.t[:, 0:1]),
                reads=(ygf, rr_), writes=(ynbt,))
            b.dma("scalar", sp_yn[tt * 128:(tt + 1) * 128, :], ynbt.ap(),
                  reads=(ynbt,), writes=(spyn_r,))
        b.barrier()
        st2.close()
        stP.close()

        if PHASE_LIMIT < 3:
            b.dma("sync", out_d.rearrange("(a p) f -> p a f", p=128), x1.ap(),
                  reads=(x1,), writes=(out_r,))
            b.barrier()
            b.flush()
            return nc
        st3 = ExitStack()
        woutb = sbt(st3, "woutb", [128, 16, DM], BF16)
        b.dma("gpsimd", woutb.ap(), WoutT_d.rearrange("(k p) f -> p k f", p=128),
              writes=(woutb,))
        nrm = sbt(st3, "nrm", [128, 16], F32)
        b.dma("sync", nrm.ap(), nrmw_d, writes=(nrm,))
        for k in range(16):
            b.op("vector", lambda e, k=k: e.tensor_scalar_mul(
                woutb.t[:, k, :], woutb.t[:, k, :], nrm.t[:, k:k + 1]),
                reads=(woutb, nrm), writes=(woutb,))
        ynf = sbt(st3, "ynf", [128, 16, TLOC], BF16)
        for di in range(16):
            b.dma("sync", ynf.t[:, di, :], sp_yn[:, di * 128:(di + 1) * 128],
                  reads=(spyn_r,), writes=(ynf,), transpose=True)
        ut = [sbt(st3, f"ut{i}", [128, DM], F32) for i in range(2)]
        m2 = sbt(st3, "m2", [128, 1], F32)
        v2 = sbt(st3, "v2", [128, 1], F32)
        s2 = sbt(st3, "s2", [128, 1], F32)
        dump2 = sbt(st3, "dump2", [128, DM], BF16)
        xfstage = sbt(st3, "xfstage", [128, DM], BF16)
        for tt in range(NTT):
            apY, bkY = pv(0, 128, 0, 1024)
            fns = []
            for di in range(16):
                for n0 in (0, 512):
                    fns.append(lambda e, apY=apY, di=di, n0=n0, tt=tt: e.matmul(
                        apY[:, n0:n0 + 512], ynf.t[:, di, tt * 128:(tt + 1) * 128],
                        woutb.t[:, di, n0:n0 + 512],
                        start=(di == 0), stop=(di == 15)))
            b.mm(fns, reads=(ynf, woutb), writes=bkY)
            utt = ut[tt % 2]
            b.dma("sync", utt.ap(), u_tok[tt * 128:(tt + 1) * 128, :], writes=(utt,))
            b.op("vector", lambda e, apY=apY, tt=tt, utt=utt: e.tensor_add(
                x1.t[:, tt, :], apY[:, 0:DM], utt.ap()),
                reads=bkY + (utt,), writes=(x1,))
            b.op("vector", lambda e, tt=tt: e.reduce_sum(
                m2.ap(), x1.t[:, tt, :], AX.X), reads=(x1,), writes=(m2,))
            b.op("vector", lambda e: e.tensor_scalar_mul(m2.ap(), m2.ap(), 1.0 / DM),
                 reads=(m2,), writes=(m2,))
            b.op("scalar", lambda e, tt=tt: e.activation(
                dump2.ap(), x1.t[:, tt, :], AF.Square, accum_out=s2.ap()),
                reads=(x1,), writes=(dump2, s2))
            b.op("vector", lambda e: e.tensor_mul(v2.ap(), m2.ap(), m2.ap()),
                 reads=(m2,), writes=(v2,))
            b.op("vector", lambda e: e.scalar_tensor_tensor(
                v2.ap(), s2.ap(), 1.0 / DM, v2.ap(), ALU.mult, ALU.subtract),
                reads=(s2, v2), writes=(v2,))
            b.op("scalar", lambda e: e.activation(v2.ap(), v2.ap(), AF.Ln,
                                                  bias=kbias.t[:, 0:1]),
                 reads=(v2, kbias), writes=(v2,))
            b.op("scalar", lambda e: e.activation(v2.ap(), v2.ap(), AF.Exp, scale=-0.5),
                 reads=(v2,), writes=(v2,))
            b.op("vector", lambda e, tt=tt: e.tensor_scalar(
                xfstage.ap(), x1.t[:, tt, :], m2.ap(), v2.ap(), ALU.subtract, ALU.mult),
                reads=(x1, m2, v2), writes=(xfstage,))
            b.dma("scalar", sp_xf[tt * 128:(tt + 1) * 128, :], xfstage.ap(),
                  reads=(xfstage,), writes=(spxf_r,))
        b.barrier()
        st3.close()

        if PHASE_LIMIT < 4:
            b.dma("sync", out_d.rearrange("(a p) f -> p a f", p=128), x1.ap(),
                  reads=(x1,), writes=(out_r,))
            b.barrier()
            b.flush()
            return nc
        st5 = ExitStack()
        wg = sbt(st5, "wg", [128, 8, DHID], BF16)
        wu = sbt(st5, "wu", [128, 8, DHID], BF16)
        wd = sbt(st5, "wd", [128, 16, DM], BF16)
        st4 = ExitStack()
        gate = sbt(st4, "gate", [128, 8, NE], BF16)
        b.dma("gpsimd", gate.ap(), gateT_d.rearrange("(k p) e -> p k e", p=128),
              writes=(gate,))
        wgv = WgT_d.rearrange("e d h -> (e d) h")
        wuv = WuT_d.rearrange("e d h -> (e d) h")
        wdv = WdT_d.rearrange("e h f -> (e h) f")
        for k in range(8):
            b.idma(wg.t[:, k, :], None, wgv, IOA(sidxi.t[:, 1 + k:2 + k], 0),
                   reads=(sidxi,), writes=(wg,))
            b.idma(wu.t[:, k, :], None, wuv, IOA(sidxi.t[:, 1 + k:2 + k], 0),
                   reads=(sidxi,), writes=(wu,))
        for dh in range(16):
            b.idma(wd.t[:, dh, :], None, wdv, IOA(sidxi.t[:, 9 + dh:10 + dh], 0),
                   reads=(sidxi,), writes=(wd,))
        xff = sbt(st4, "xff", [128, 8, TLOC], BF16)
        for k in range(8):
            b.dma("sync", xff.t[:, k, :], sp_xf[:, k * 128:(k + 1) * 128],
                  reads=(spxf_r,), writes=(xff,), transpose=True)
        TBE = NE * NSLOTP * 2
        ztb = sbt(st4, "ztb", [128, TBE // 128], F32)
        b.op("vector", lambda e: e.memset(ztb.ap(), 0.0), writes=(ztb,))
        b.dma("scalar", tbl_d.rearrange("a c -> (a c)")
              .rearrange("(p f) -> p f", p=128), ztb.ap(),
              reads=(ztb,), writes=(tbl_r,))
        b.cc("AllGather", ALU.bypass, [[0, 1, 2, 3, 4, 5, 6, 7]],
             sp_xf, agxf_out, reads=(spxf_r,), writes=(agxf_r,))
        probs = sbt(st4, "probs", [128, NE], F32)
        nmx = sbt(st4, "nmx", [128, 1], F32)
        sume = sbt(st4, "sume", [128, 1], F32)
        m1 = sbt(st4, "m1", [128, 1], F32)
        mm2 = sbt(st4, "mm2", [128, 1], F32)
        oh1 = sbt(st4, "oh1", [128, NE], F32)
        oh2 = sbt(st4, "oh2", [128, NE], F32)
        csx = sbt(st4, "csx", [128, NE], F32)
        zer = sbt(st4, "zer", [128, NE], F32)
        p2_ = sbt(st4, "p2_", [128, NE], F32)
        w12 = sbt(st4, "w12", [128, 2], F32)
        ohb = sbt(st4, "ohb", [128, NE], BF16)
        Pl = sbt(st4, "Pl", [128, NTT, 2, NE], F32)
        totf = sbt(st4, "totf", [1, 16 * NTT], F32)
        keepm = sbt(st4, "keepm", [128, NE], F32)
        comb = sbt(st4, "comb", [128, NTT, NE], F32)
        comb2 = sbt(st4, "comb2", [128, NTT, NE], F32)
        b.op("vector", lambda e: e.memset(zer.ap(), 0.0), writes=(zer,))

        for tt in range(NTT):
            apL, bkL = pv(0, 128, 0, 8)
            b.mm([lambda e, apL=apL, k=k, tt=tt: e.matmul(
                apL[:, 0:NE], xff.t[:, k, tt * 128:(tt + 1) * 128], gate.t[:, k, :],
                start=(k == 0), stop=(k == 7)) for k in range(8)],
                reads=(xff, gate), writes=bkL)
            b.op("vector", lambda e, apL=apL: e.reduce_max(nmx.ap(), apL[:, 0:NE], AX.X, negate=True),
                 reads=bkL, writes=(nmx,))
            b.op("scalar", lambda e, apL=apL: e.activation(
                probs.ap(), apL[:, 0:NE], AF.Exp, bias=nmx.ap(), accum_out=sume.ap()),
                reads=bkL + (nmx,), writes=(probs, sume))
            b.op("vector", lambda e: e.reciprocal(sume.ap(), sume.ap()),
                 reads=(sume,), writes=(sume,))
            b.op("vector", lambda e: e.tensor_scalar_mul(probs.ap(), probs.ap(), sume.ap()),
                 reads=(probs, sume), writes=(probs,))
            b.op("vector", lambda e: e.reduce_max(m1.ap(), probs.ap(), AX.X),
                 reads=(probs,), writes=(m1,))
            b.op("vector", lambda e: e.tensor_single_scalar(
                oh1.ap(), probs.ap(), m1.ap(), ALU.is_ge), reads=(probs, m1), writes=(oh1,))
            b.op("vector", lambda e: e.tensor_tensor_scan(
                csx.ap(), oh1.ap(), zer.ap(), 0.0, ALU.add, ALU.add),
                reads=(oh1, zer), writes=(csx,))
            b.op("vector", lambda e: e.tensor_single_scalar(
                csx.ap(), csx.ap(), 1.5, ALU.is_le), reads=(csx,), writes=(csx,))
            b.op("vector", lambda e: e.tensor_mul(oh1.ap(), oh1.ap(), csx.ap()),
                 reads=(oh1, csx), writes=(oh1,))
            b.op("vector", lambda e: e.tensor_scalar(
                csx.ap(), oh1.ap(), -1.0, 1.0, ALU.mult, ALU.add),
                reads=(oh1,), writes=(csx,))
            b.op("vector", lambda e: e.tensor_mul(p2_.ap(), probs.ap(), csx.ap()),
                 reads=(probs, csx), writes=(p2_,))
            b.op("vector", lambda e: e.reduce_max(mm2.ap(), p2_.ap(), AX.X),
                 reads=(p2_,), writes=(mm2,))
            b.op("vector", lambda e: e.tensor_single_scalar(
                oh2.ap(), p2_.ap(), mm2.ap(), ALU.is_ge), reads=(p2_, mm2), writes=(oh2,))
            b.op("vector", lambda e: e.tensor_tensor_scan(
                csx.ap(), oh2.ap(), zer.ap(), 0.0, ALU.add, ALU.add),
                reads=(oh2, zer), writes=(csx,))
            b.op("vector", lambda e: e.tensor_single_scalar(
                csx.ap(), csx.ap(), 1.5, ALU.is_le), reads=(csx,), writes=(csx,))
            b.op("vector", lambda e: e.tensor_mul(oh2.ap(), oh2.ap(), csx.ap()),
                 reads=(oh2, csx), writes=(oh2,))
            b.op("vector", lambda e: e.tensor_add(w12.t[:, 0:1], m1.ap(), mm2.ap()),
                 reads=(m1, mm2), writes=(w12,))
            b.op("vector", lambda e: e.tensor_scalar_add(
                w12.t[:, 0:1], w12.t[:, 0:1], 1e-9), reads=(w12,), writes=(w12,))
            b.op("vector", lambda e: e.reciprocal(w12.t[:, 1:2], w12.t[:, 0:1]),
                 reads=(w12,), writes=(w12,))
            b.op("vector", lambda e: e.tensor_mul(m1.ap(), m1.ap(), w12.t[:, 1:2]),
                 reads=(m1, w12), writes=(m1,))
            b.op("vector", lambda e: e.tensor_mul(mm2.ap(), mm2.ap(), w12.t[:, 1:2]),
                 reads=(mm2, w12), writes=(mm2,))
            for kk, oh in ((0, oh1), (1, oh2)):
                b.op("vector", lambda e, oh=oh: e.tensor_copy(ohb.ap(), oh.ap()),
                     reads=(oh,), writes=(ohb,))
                apP, bkP = pv(0, 128, 512, 8)
                b.mm(lambda e, apP=apP: e.matmul(apP[:, 0:NE], tri128.ap(), ohb.ap(),
                                        start=True, stop=True),
                     reads=(tri128, ohb), writes=bkP)
                b.op("vector", lambda e, apP=apP, tt=tt, kk=kk: e.tensor_copy(
                    Pl.t[:, tt, kk, :], apP[:, 0:NE]), reads=bkP, writes=(Pl,))
                apt, bkt = pv(0, 1, 1024, 8)
                b.mm(lambda e, apt=apt: e.matmul(apt[:, 0:NE], ones128[:, 0:1], ohb.ap(),
                                        start=True, stop=True),
                     reads=(ones128, ohb), writes=bkt)
                b.op("vector", lambda e, apt=apt, tt=tt, kk=kk: e.tensor_copy(
                    totf.t[:, tt * 16 + kk * 8: tt * 16 + kk * 8 + 8], apt[:, 0:NE]),
                    reads=bkt, writes=(totf,))
            b.op("vector", lambda e: e.tensor_scalar_mul(oh1.ap(), oh1.ap(), m1.ap()),
                 reads=(oh1, m1), writes=(oh1,))
            b.op("vector", lambda e: e.tensor_scalar_mul(oh2.ap(), oh2.ap(), mm2.ap()),
                 reads=(oh2, mm2), writes=(oh2,))
            b.op("vector", lambda e, tt=tt: e.tensor_copy(comb.t[:, tt, 0:NE], oh1.ap()),
                 reads=(oh1,), writes=(comb,))
            b.op("vector", lambda e, tt=tt: e.tensor_copy(comb2.t[:, tt, 0:NE], oh2.ap()),
                 reads=(oh2,), writes=(comb2,))

        b.dma("scalar", sp_rt[:, :], totf.ap(), reads=(totf,), writes=(sprt_r,))
        tot2 = sbt(st4, "tot2", [NTT, 16], F32)
        b.dma("scalar", tot2.ap(), sp_rt.rearrange("o (a b) -> (o a) b", b=16),
              reads=(sprt_r,), writes=(tot2,))
        apO, bkO = pv(0, NTT, 0, 16)
        b.mm(lambda e: e.matmul(apO[:, 0:16], triNT.ap(), tot2.ap(),
                                start=True, stop=True),
             reads=(triNT, tot2), writes=bkO)
        offs = sbt(st4, "offs", [NTT, 16], F32)
        b.op("vector", lambda e: e.tensor_copy(offs.ap(), apO[:, 0:16]),
             reads=bkO, writes=(offs,))
        apC, bkC = pv(0, 1, 512, 16)
        b.mm(lambda e: e.matmul(apC[:, 0:16], ocNT.ap(), tot2.ap(),
                                start=True, stop=True),
             reads=(ocNT, tot2), writes=bkC)
        ctot = sbt(st4, "ctot", [1, 16], F32)
        b.op("vector", lambda e: e.tensor_copy(ctot.ap(), apC[:, 0:16]),
             reads=bkC, writes=(ctot,))
        b.dma("scalar", agc_in[:, :], ctot.ap(), reads=(ctot,), writes=(agci_r,))
        b.cc("AllGather", ALU.bypass, [[0, 1, 2, 3, 4, 5, 6, 7]],
             agc_in, agc_out, reads=(agci_r,), writes=(agco_r,))
        cnt8 = sbt(st4, "cnt8", [8, 16], F32)
        b.dma("sync", cnt8.ap(), agc_out, reads=(agco_r,), writes=(cnt8,))
        apGO, bkGO = pv(0, 1, 1024, 16)
        b.mm(lambda e: e.matmul(apGO[:, 0:16], cst.t[0:8, 1:2], cnt8.ap(),
                                start=True, stop=True),
             reads=(cst, cnt8), writes=bkGO)
        glob = sbt(st4, "glob", [1, 16], F32)
        b.op("vector", lambda e: e.tensor_copy(glob.ap(), apGO[:, 0:16]),
             reads=bkGO, writes=(glob,))
        b.dma("scalar", sp_ro[:, :], offs.ap(), reads=(offs,), writes=(spro_r,))
        offl = sbt(st4, "offl", [1, 16 * NTT], F32)
        b.dma("scalar", offl.ap(), sp_ro.rearrange("(o a) b -> o (a b)", o=1),
              reads=(spro_r,), writes=(offl,))
        gvv = glob.t[:, :].unsqueeze(1).broadcast_to([1, NTT, 16])
        b.op("vector", lambda e, gvv=gvv: e.tensor_tensor(
            offl.ap().rearrange("o (a c) -> o a c", c=16),
            offl.ap().rearrange("o (a c) -> o a c", c=16), gvv, ALU.add),
            reads=(offl, glob), writes=(offl,))
        ohi = sbt(st4, "ohi", [128, NE], F32)
        tm8 = sbt(st4, "tm8", [128, NE], F32)
        grs = sbt(st4, "grs", [128, 1], F32)
        eix = sbt(st4, "eix", [128, 1], F32)
        kfl = sbt(st4, "kfl", [128, 1], F32)
        flt = sbt(st4, "flt", [128, 1], F32)
        vvb = [sbt(st4, f"vv{i}", [128, 2], F32) for i in range(2)]
        ixsb = [sbt(st4, f"ixs{i}", [128, 1], I32) for i in range(2)]
        for tt in range(NTT):
            for kk in range(2):
                src = comb if kk == 0 else comb2
                ap2, bk2 = pv(0, 128, 512, 8)
                b.mm(lambda e, ap2=ap2, tt=tt, kk=kk: e.matmul(
                    ap2[:, 0:NE], or128.ap(),
                    bass.AP(offl.t, tt * 16 + kk * 8, [[16 * NTT, 1], [1, 8]]),
                    start=True, stop=True),
                    reads=(or128, offl), writes=bk2)
                b.op("vector", lambda e, ap2=ap2, tt=tt, kk=kk: e.tensor_add(
                    keepm.ap(), Pl.t[:, tt, kk, :], ap2[:, 0:NE]),
                    reads=(Pl,) + bk2, writes=(keepm,))
                b.op("vector", lambda e, tt=tt, src=src: e.tensor_single_scalar(
                    ohi.ap(), src.t[:, tt, :], 0.0, ALU.is_gt),
                    reads=(src,), writes=(ohi,))
                b.op("vector", lambda e: e.tensor_mul(tm8.ap(), keepm.ap(), ohi.ap()),
                     reads=(keepm, ohi), writes=(tm8,))
                b.op("vector", lambda e: e.reduce_sum(grs.ap(), tm8.ap(), AX.X),
                     reads=(tm8,), writes=(grs,))
                b.op("vector", lambda e: e.tensor_single_scalar(
                    keepm.ap(), keepm.ap(), float(CAP), ALU.is_lt),
                    reads=(keepm,), writes=(keepm,))
                b.op("vector", lambda e: e.tensor_mul(tm8.ap(), keepm.ap(), ohi.ap()),
                     reads=(keepm, ohi), writes=(tm8,))
                b.op("vector", lambda e: e.reduce_sum(kfl.ap(), tm8.ap(), AX.X),
                     reads=(tm8,), writes=(kfl,))
                b.op("vector", lambda e: e.tensor_mul(tm8.ap(), erow.ap(), ohi.ap()),
                     reads=(erow, ohi), writes=(tm8,))
                b.op("vector", lambda e: e.reduce_sum(eix.ap(), tm8.ap(), AX.X),
                     reads=(tm8,), writes=(eix,))
                b.op("vector", lambda e: e.tensor_scalar_add(
                    flt.ap(), grs.ap(), float(-CAPP)),
                    reads=(grs,), writes=(flt,))
                b.op("vector", lambda e: e.tensor_mul(flt.ap(), flt.ap(), kfl.ap()),
                     reads=(flt, kfl), writes=(flt,))
                b.op("vector", lambda e: e.tensor_scalar_add(
                    flt.ap(), flt.ap(), float(CAPP)),
                    reads=(flt,), writes=(flt,))
                b.op("vector", lambda e: e.tensor_add(flt.ap(), flt.ap(), eix.ap()),
                     reads=(flt, eix), writes=(flt,))
                b.op("vector", lambda e, tt=tt, kk=kk: e.tensor_copy(
                    idxg.t[:, tt, kk:kk + 1], flt.ap()),
                    reads=(flt,), writes=(idxg,))
                b.op("vector", lambda e: e.tensor_add(flt.ap(), flt.ap(), eix.ap()),
                     reads=(flt, eix), writes=(flt,))
                b.op("vector", lambda e, kk=kk: e.tensor_scalar_add(
                    flt.ap(), flt.ap(), float(kk * CAPQ)),
                    reads=(flt,), writes=(flt,))
                ixs = ixsb[kk]
                b.op("vector", lambda e, ixs=ixs: e.tensor_copy(ixs.ap(), flt.ap()),
                     reads=(flt,), writes=(ixs,))
                vv = vvb[kk]
                b.op("vector", lambda e, tt=tt, vv=vv: e.tensor_copy(
                    vv[:, 0:1], gtb.t[:, tt:tt + 1]), reads=(gtb,), writes=(vv,))
                b.op("vector", lambda e, tt=tt, src=src, vv=vv: e.reduce_sum(
                    vv[:, 1:2], src.t[:, tt, :], AX.X), reads=(src,), writes=(vv,))
                b.op("vector", lambda e, vv=vv: e.tensor_mul(
                    vv[:, 1:2], vv[:, 1:2], kfl.ap()), reads=(vv, kfl), writes=(vv,))
                b.idma(tbl_d, IOA(ixs.ap(), 0), vv.ap(), None,
                       reads=(ixs, vv), writes=(tbl_r,))
        b.cc("ReduceScatter", ALU.add, [[0, 1, 2, 3, 4, 5, 6, 7]],
             tbl_d, tmn_d, reads=(tbl_r,), writes=(tmn_r,))
        b.barrier()
        st4.close()

        if PHASE_LIMIT < 5:
            b.dma("sync", out_d.rearrange("(a p) f -> p a f", p=128), x1.ap(),
                  reads=(x1,), writes=(out_r,))
            b.barrier()
            b.flush()
            return nc
        Hb = sbt(st5, "Hb", [128, 16, 512], BF16)
        xqb = [sbt(st5, f"xq{i}", [128, 8, 512], BF16) for i in range(2)]
        sg = [sbt(st5, "sg0", [128, 512], BF16)] * 2
        xgab = [sbt(st5, "xga0", [128, DM], BF16)] * 2
        ysbb = [sbt(st5, f"ysb{i}", [128, DM], BF16) for i in range(2)]
        grb = [sbt(st5, f"gr{i}", [128, DM], BF16) for i in range(2)]
        tbn = sbt(st5, "tbn", [128, 2, NJK, 2], F32)
        tki = sbt(st5, "tki", [128, 2, NJK], I32)

        for kb in range(2):
            b.dma("gpsimd", tbn.t[:, kb, :, :],
                  tmn_d[kb * CAPQ:kb * CAPQ + CAPP]
                  .rearrange("(j p) c -> p j c", p=128),
                  reads=(tmn_r,), writes=(tbn,))
        zpad = ysbb[0]
        padR = Res("agy_pad")
        b.op("vector", lambda e, zpad=zpad: e.memset(zpad.ap(), 0.0),
             writes=(zpad,))
        for kb in range(2):
            b.dma("gpsimd", agy_in[kb * CAPQ + CAPP:(kb + 1) * CAPQ, :], zpad.ap(),
                  reads=(zpad,), writes=(padR,))
        b.op("vector", lambda e: e.tensor_copy(tki.ap(), tbn.t[:, :, :, 0]),
             reads=(tbn,), writes=(tki,))

        outv = out_d.rearrange("(a p) f -> p a f", p=128)

        def emit_home(tt, kk):
            gr = grb[tt % 2]
            b.idma(gr.ap(), None, agy_o[kk], IOA(idxg.t[:, tt, kk:kk + 1], 0),
                   reads=(agyoR[kk], idxg), writes=(gr,))
            b.op("vector", lambda e, tt=tt, gr=gr: e.tensor_add(
                x1.t[:, tt, :], x1.t[:, tt, :], gr.ap()),
                reads=(gr, x1), writes=(x1,))
            if kk == 1:
                b.dma("sync", outv[:, tt, :], x1.t[:, tt, :],
                      reads=(x1,), writes=(out_r,))

        for kb in range(2):
            for tq in range(NQK):
                gtq = kb * NQK + tq
                for ji in range(4):
                    jj = tq * 4 + ji
                    xga = xgab[jj % 2]
                    b.idma(xga.ap(), None, agxf_out,
                           IOA(tki.t[:, kb, jj:jj + 1], 0),
                           reads=(agxf_r, tki), writes=(xga,))
                    b.dma("gpsimd",
                          xe_d[(gtq * 4 + ji) * 128:(gtq * 4 + ji + 1) * 128, :],
                          xga.ap(), reads=(xga,), writes=(xeR[gtq % 2],))
                if kb == 1 and tq == 0:
                    b.cc("AllGather", ALU.bypass, [[0, 1, 2, 3, 4, 5, 6, 7]],
                         agy_in[0:CAPQ, :], agy_o[0],
                         reads=(agyiR[0], padR), writes=(agyoR[0],))
                xq = xqb[gtq % 2]
                for d in range(8):
                    b.dma("sync", xq.t[:, d, :],
                          xe_d[gtq * 512:(gtq + 1) * 512, d * 128:(d + 1) * 128],
                          reads=(xeR[gtq % 2],), writes=(xq,), transpose=True)
                for dh in range(16):
                    apGm, bkGm = pv(0, 128, 512 * (dh % 2), 512)
                    apUm, bkUm = pv(0, 128, 1024 + 512 * (dh % 2), 512)
                    b.mm([lambda e, k=k, dh=dh, xq=xq, apGm=apGm: e.matmul(
                        apGm[:, 0:512], wg.t[:, k, dh * 128:(dh + 1) * 128],
                        xq.t[:, k, :], start=(k == 0), stop=(k == 7))
                        for k in range(8)],
                        reads=(wg, xq), writes=bkGm)
                    b.mm([lambda e, k=k, dh=dh, xq=xq, apUm=apUm: e.matmul(
                        apUm[:, 0:512], wu.t[:, k, dh * 128:(dh + 1) * 128],
                        xq.t[:, k, :], start=(k == 0), stop=(k == 7))
                        for k in range(8)],
                        reads=(wu, xq), writes=bkUm)
                    sgt = sg[dh % 2]
                    b.op("scalar", lambda e, sgt=sgt, apGm=apGm: e.activation(
                        sgt.ap(), apGm[:, 0:512], AF.Silu),
                        reads=bkGm, writes=(sgt,))
                    b.op("vector", lambda e, dh=dh, sgt=sgt, apUm=apUm: e.tensor_tensor(
                        Hb.t[:, dh, :], sgt.ap(), apUm[:, 0:512], ALU.mult),
                        reads=(sgt,) + bkUm, writes=(Hb,))
                for ji in range(4):
                    apYm, bkYm = pv(0, 128, 1024 * ji, 1024)
                    fns = []
                    for dh in range(16):
                        for n0 in (0, 512):
                            fns.append(lambda e, dh=dh, ji=ji, n0=n0, apYm=apYm:
                                       e.matmul(
                                           apYm[:, n0:n0 + 512],
                                           Hb.t[:, dh, ji * 128:(ji + 1) * 128],
                                           wd.t[:, dh, n0:n0 + 512],
                                           start=(dh == 0), stop=(dh == 15)))
                    b.mm(fns, reads=(Hb, wd), writes=bkYm)
                for ji in range(4):
                    jj = tq * 4 + ji
                    apYm, bkYm = pv(0, 128, 1024 * ji, 1024)
                    ysb = ysbb[ji % 2]
                    b.op("vector", lambda e, kb=kb, jj=jj, apYm=apYm, ysb=ysb:
                         e.tensor_scalar_mul(
                             ysb.ap(), apYm[:, 0:DM], tbn.t[:, kb, jj, 1:2]),
                         reads=bkYm + (tbn,), writes=(ysb,))
                    b.dma("scalar",
                          agy_in[kb * CAPQ + jj * 128:kb * CAPQ + (jj + 1) * 128, :],
                          ysb.ap(), reads=(ysb,), writes=(agyiR[kb],))
        b.cc("AllGather", ALU.bypass, [[0, 1, 2, 3, 4, 5, 6, 7]],
             agy_in[CAPQ:2 * CAPQ, :], agy_o[1],
             reads=(agyiR[1], padR), writes=(agyoR[1],))
        for tt in range(NTT):
            emit_home(tt, 0)
        for tt in range(NTT):
            emit_home(tt, 1)
        st5.close()

        b.barrier()
        b.flush()
    return nc



def kernel(L0_attn_u, L0_attn_W_in, L0_attn_conv_w, L0_attn_conv_b,
           L0_attn_dt_bias, L0_attn_A_log, L0_attn_D, L0_attn_norm_w,
           L0_attn_W_out, L0_ffn_gate_w, L0_ffn_gate_proj_w, L0_ffn_up_proj_w,
           L0_ffn_down_proj_w):
    u = np.asarray(L0_attn_u, dtype=np.float32)
    Bsz, S, _ = u.shape
    assert Bsz == 2
    TLOC = S // 4
    nc = build_program(TLOC)

    WinT = np.ascontiguousarray(np.asarray(L0_attn_W_in, np.float32).T)
    convw = np.ascontiguousarray(
        np.asarray(L0_attn_conv_w, np.float32).reshape(NXBC, 128, 4).transpose(1, 0, 2))
    convb = np.ascontiguousarray(
        np.asarray(L0_attn_conv_b, np.float32).reshape(NXBC, 128).T)
    dtb = np.ascontiguousarray(np.asarray(L0_attn_dt_bias, np.float32).reshape(NH, 1))
    Alog = np.ascontiguousarray(np.asarray(L0_attn_A_log, np.float32).reshape(NH, 1))
    Dhp = np.ascontiguousarray(np.repeat(np.asarray(L0_attn_D, np.float32), HD)[None, :])
    nrmw = np.ascontiguousarray(
        np.asarray(L0_attn_norm_w, np.float32).reshape(16, 128).T)
    WoutT = np.ascontiguousarray(np.asarray(L0_attn_W_out, np.float32).T)
    gateT = np.ascontiguousarray(np.asarray(L0_ffn_gate_w, np.float32).T)
    WgT = np.ascontiguousarray(np.asarray(L0_ffn_gate_proj_w, np.float32).transpose(0, 2, 1))
    WuT = np.ascontiguousarray(np.asarray(L0_ffn_up_proj_w, np.float32).transpose(0, 2, 1))
    WdT = np.ascontiguousarray(np.asarray(L0_ffn_down_proj_w, np.float32).transpose(0, 2, 1))

    CAP = int(CAPF * (8 * TLOC) / NE)
    CAPP = 512 * ((CAP + 511) // 512)
    NSLOTP = 2 * (CAPP + 128)
    SPPP = NSLOTP // 128
    NTT = TLOC // 128
    pp = np.arange(128, dtype=np.float32)

    in_maps = []
    for c in range(8):
        bi, q = divmod(c, 4)
        s0 = q * TLOC
        useg = u[bi, s0:s0 + TLOC]
        gtok = (c * TLOC + 128.0 * np.arange(NTT, dtype=np.float32)[None, :]
                + pp[:, None]).astype(np.float32)
        sidx = np.zeros((128, 26), np.float32)
        sidx[:, 0] = c * NSLOTP + SPPP * pp
        for k in range(8):
            sidx[:, 1 + k] = c * DM + 128 * k + pp
        for dh in range(16):
            sidx[:, 9 + dh] = c * DHID + 128 * dh + pp
        halo = np.zeros((4, DM), np.float32) if q == 0 else u[bi, s0 - 4:s0]
        uTh = np.ascontiguousarray(np.concatenate([halo, useg], 0).T)
        cc = np.zeros((64, 16), np.float32)
        for j in range(8):
            cc[j, 0] = 1.0 if (j // 4 == bi and j < c) else 0.0
            cc[j, 1] = 1.0 if j < c else 0.0
        for k in range(8):
            for j in range(8):
                if j < k < c and k // 4 == bi and j // 4 == bi:
                    cc[k, 2 + j] = 1.0
        in_maps.append({
            "uT": uTh, "u_tok": np.ascontiguousarray(useg), "cconst": cc,
            "WinT": WinT, "convw": convw, "convb": convb, "dtb": dtb,
            "Alog": Alog, "Dhp": Dhp, "nrmw": nrmw, "WoutT": WoutT,
            "gateT": gateT, "WgT": WgT, "WuT": WuT, "WdT": WdT,
            "gtok": gtok, "sidx": sidx,
        })

    res = run_bass_kernel_spmd(nc, in_maps, core_ids=list(range(8)))
    full = np.empty((2, S, DM), np.float32)
    for c in range(8):
        bi, q = divmod(c, 4)
        full[bi, q * TLOC:(q + 1) * TLOC] = res.results[c]["out"]
    return full


# version 20 (final)
# speedup vs baseline: 1.0140x; 1.0140x over previous
import os
import numpy as np
import ml_dtypes
from contextlib import ExitStack

import concourse.bass as bass
import concourse.mybir as mybir
from concourse.bass_utils import run_bass_kernel_spmd

F32 = mybir.dt.float32
BF16 = mybir.dt.bfloat16
I32 = mybir.dt.int32
IOA = bass.IndirectOffsetOnAxis
AF = mybir.ActivationFunctionType
ALU = mybir.AluOpType
AX = mybir.AxisListType
bfnp = ml_dtypes.bfloat16

DM = 1024
DIN = 2048
DST = 128
NH = 32
HD = 64
CK = 64
DPROJ = 2 * DIN + 2 * DST + NH
NXBC = 18
NE = 8
DHID = 2048
CAPF = 1.25
EPS = 1e-5

ENGSEM = {"tensor": "pe", "vector": "dv", "scalar": "ac", "gpsimd": "gp"}
QSEM = {"sync": "qs", "scalar": "qa", "gpsimd": "qg"}
SKIP_SAME_ENG = os.environ.get("KQ_SIM_SAFE", "0") != "1"
SKIP_SEMS = set(os.environ.get("KQ_SKIP_SEMS", "").split(","))
PHASE_LIMIT = int(os.environ.get("KQ_PHASE", "9"))
NO_U = os.environ.get("KQ_NOU", "0") == "1"
NO_AG = os.environ.get("KQ_NOAG", "0") == "1"
NO_TR = os.environ.get("KQ_NOTR", "0") == "1"


class Res:
    __slots__ = ("w", "r", "name")

    def __init__(self, name=""):
        self.w = None
        self.r = {}
        self.name = name


class TB:
    __slots__ = ("t", "res")

    def __init__(self, t, name=""):
        self.t = t
        self.res = Res(name)

    def __getitem__(self, key):
        return self.t[key]

    def ap(self):
        return self.t[:]


def _res_of(x):
    return x.res if isinstance(x, TB) else x


class Bld:

    def __init__(self, nc, block):
        self.nc = nc
        self.block = block
        self.ops = {e: [] for e in ("sync", "tensor", "vector", "scalar", "gpsimd")}
        self.cnt = {s: 0 for s in ("pe", "dv", "ac", "gp", "cc")}
        self.S = {}
        self.waited = {}
        self.dsem = {}
        self.stack = None

    def setup(self, stack):
        self.stack = stack
        for s in list(self.cnt):
            self.S[s] = stack.enter_context(self.nc.semaphore(s))

    def _dma_sem(self, res):
        key = id(res)
        if key not in self.dsem:
            name = f"dq{len(self.dsem)}_{res.name[:12]}"
            self.S[name] = self.stack.enter_context(self.nc.semaphore(name))
            self.cnt[name] = 0
            self.dsem[key] = name
        return self.dsem[key]

    def _emit(self, eng, f):
        self.ops[eng].append(f)

    def _wait(self, eng, sem, val, allow_skip=False):
        if (allow_skip and SKIP_SAME_ENG and ENGSEM.get(eng) == sem
                and sem in SKIP_SEMS):
            return
        key = (eng, sem)
        if self.waited.get(key, 0) >= val:
            return
        self.waited[key] = val
        S = self.S[sem]
        self._emit(eng, lambda e: e.wait_ge(S, val))

    def _deps(self, eng, reads, writes, allow_skip=False):
        for r in reads:
            rr = _res_of(r)
            if rr.w:
                self._wait(eng, *rr.w, allow_skip=allow_skip)
        for w in writes:
            wr = _res_of(w)
            if wr.w:
                self._wait(eng, *wr.w, allow_skip=allow_skip)
            for sem, cv in wr.r.items():
                self._wait(eng, sem, cv, allow_skip=allow_skip)

    def _mark(self, reads, writes, sem, n):
        for w in writes:
            wr = _res_of(w)
            wr.w = (sem, n)
            wr.r = {}
        for r in reads:
            rr = _res_of(r)
            rr.r[sem] = max(rr.r.get(sem, 0), n)

    def op(self, eng, fn, reads=(), writes=()):
        self._deps(eng, reads, writes, allow_skip=True)
        sem = ENGSEM[eng]
        self.cnt[sem] += 1
        n = self.cnt[sem]
        S = self.S[sem]
        self._emit(eng, lambda e: fn(e).then_inc(S, 1))
        self._mark(reads, writes, sem, n)

    def mm(self, fns, reads=(), writes=()):
        if callable(fns):
            fns = [fns]
        self._deps("tensor", reads, writes, allow_skip=True)
        self.cnt["pe"] += 1
        n = self.cnt["pe"]
        S = self.S["pe"]

        def body(e):
            for i, fn in enumerate(fns):
                inst = fn(e)
                if i == len(fns) - 1:
                    inst.then_inc(S, 1)
        self._emit("tensor", body)
        self._mark(reads, writes, "pe", n)

    def dma(self, q, dst, src, reads=(), writes=(), transpose=False, slow=False):
        self._deps(q, reads, writes)
        assert writes, "dma needs a write Res for its semaphore"
        sem = self._dma_sem(_res_of(writes[0]))
        self.cnt[sem] += 16
        n = self.cnt[sem]
        S = self.S[sem]
        if transpose:
            self._emit(q, lambda e: e.dma_start(dst, src, transpose=True).then_inc(S, 16))
        elif slow:
            self._emit(q, lambda e: e.dma_start(
                dst, src, allow_slow_non_contiguous=True).then_inc(S, 16))
        else:
            self._emit(q, lambda e: e.dma_start(dst, src).then_inc(S, 16))
        self._mark(reads, writes, sem, n)

    def idma(self, dst, dst_off, src, src_off, reads=(), writes=(), bounds=None):
        self._deps("gpsimd", reads, writes)
        assert writes
        sem = self._dma_sem(_res_of(writes[0]))
        self.cnt[sem] += 16
        n = self.cnt[sem]
        S = self.S[sem]
        if bounds is None:
            self._emit("gpsimd", lambda e: e.indirect_dma_start(
                dst, dst_off, src, src_off).then_inc(S, 16))
        else:
            self._emit("gpsimd", lambda e: e.indirect_dma_start(
                dst, dst_off, src, src_off, bounds_check=bounds,
                oob_is_err=False).then_inc(S, 16))
        self._mark(reads, writes, sem, n)

    def cc(self, kind, alu, groups, in_ap, out_ap, reads=(), writes=()):
        self._deps("gpsimd", reads, writes)
        self.cnt["cc"] += 1
        n = self.cnt["cc"]
        S = self.S["cc"]
        self._emit("gpsimd", lambda e: e.collective_compute(
            kind, alu, replica_groups=groups, ins=[in_ap], outs=[out_ap],
        ).then_inc(S, 1))
        self._mark(reads, writes, "cc", n)

    def barrier(self):
        tgt = dict(self.cnt)
        for eng in ("sync", "tensor", "vector", "scalar", "gpsimd"):
            for sem, val in tgt.items():
                if val:
                    self._wait(eng, sem, val)

    def flush(self):
        for eng, lst in self.ops.items():
            def runner(e, lst=lst):
                for f in lst:
                    f(e)
            getattr(self.block, eng)(runner)


def hview(ap2d, P, reps):
    return ap2d.unsqueeze(2).broadcast_to([P, NH, reps])


def build_program(TLOC):
    assert TLOC % 128 == 0
    NCK = TLOC // CK
    NTT = TLOC // 128
    CAP = int(CAPF * (8 * TLOC) / NE)
    CAPP = 512 * ((CAP + 511) // 512)
    CAPQ = CAPP + 128
    NSLOTP = 2 * CAPQ
    SPPP = NSLOTP // 128
    NJK = CAPP // 128
    NQK = CAPP // 512
    blocks = []
    col = 0
    while col < TLOC:
        w = min(516, 4 + TLOC - col)
        blocks.append((col, w))
        col += w - 4
    NBLK = len(blocks)
    NQ = (TLOC + 511) // 512

    nc = bass.Bass("TRN2", target_bir_lowering=False)

    uT = nc.dram_tensor("uT", [DM, 4 + TLOC], F32, kind="ExternalInput").ap()
    u_tok = nc.dram_tensor("u_tok", [TLOC, DM], F32, kind="ExternalInput").ap()
    cconst = nc.dram_tensor("cconst", [64, 16], F32, kind="ExternalInput").ap()
    WinT_d = nc.dram_tensor("WinT", [DM, DPROJ], F32, kind="ExternalInput").ap()
    convw_d = nc.dram_tensor("convw", [128, NXBC, 4], F32, kind="ExternalInput").ap()
    convb_d = nc.dram_tensor("convb", [128, NXBC], F32, kind="ExternalInput").ap()
    dtb_d = nc.dram_tensor("dtb", [NH, 1], F32, kind="ExternalInput").ap()
    Alog_d = nc.dram_tensor("Alog", [NH, 1], F32, kind="ExternalInput").ap()
    Dhp_d = nc.dram_tensor("Dhp", [1, DIN], F32, kind="ExternalInput").ap()
    nrmw_d = nc.dram_tensor("nrmw", [128, 16], F32, kind="ExternalInput").ap()
    WoutT_d = nc.dram_tensor("WoutT", [DIN, DM], F32, kind="ExternalInput").ap()
    gateT_d = nc.dram_tensor("gateT", [DM, NE], F32, kind="ExternalInput").ap()
    gtok_d = nc.dram_tensor("gtok", [128, NTT], F32, kind="ExternalInput").ap()
    sidx_d = nc.dram_tensor("sidx", [128, 26], F32, kind="ExternalInput").ap()
    WgT_d = nc.dram_tensor("WgT", [NE, DM, DHID], F32, kind="ExternalInput").ap()
    WuT_d = nc.dram_tensor("WuT", [NE, DM, DHID], F32, kind="ExternalInput").ap()
    WdT_d = nc.dram_tensor("WdT", [NE, DHID, DM], F32, kind="ExternalInput").ap()
    out_d = nc.dram_tensor("out", [TLOC, DM], F32, kind="ExternalOutput").ap()

    sp_x = nc.dram_tensor("sp_x", [DIN, TLOC], BF16).ap()
    sp_z = nc.dram_tensor("sp_z", [DIN, TLOC], BF16).ap()
    sp_B = nc.dram_tensor("sp_B", [DST, TLOC], BF16).ap()
    sp_cs = nc.dram_tensor("sp_cs", [NH, TLOC], F32).ap()
    sp_pd = nc.dram_tensor("sp_pd", [NH, 9 * NBLK], F32).ap()
    sp_yn = nc.dram_tensor("sp_yn", [TLOC, DIN], BF16).ap()
    sp_xf = nc.dram_tensor("sp_xf", [TLOC, DM], BF16).ap()
    sp_wf = nc.dram_tensor("sp_wf", [8, NH], F32).ap()
    sp_rt = nc.dram_tensor("sp_rt", [1, 16 * NTT], F32).ap()
    sp_ro = nc.dram_tensor("sp_ro", [NTT, 16], F32).ap()
    ag_in = nc.dram_tensor("ag_in", [129, DIN], F32).ap()
    ag_out = nc.dram_tensor("ag_out", [8 * 129, DIN], F32, addr_space="Shared").ap()
    agc_in = nc.dram_tensor("agc_in", [1, 16], F32).ap()
    agc_out = nc.dram_tensor("agc_out", [8, 16], F32, addr_space="Shared").ap()
    tbl_d = nc.dram_tensor("tbl", [NE * NSLOTP, 2], F32).ap()
    tmn_d = nc.dram_tensor("tmn", [NSLOTP, 2], F32).ap()
    xe_d = nc.dram_tensor("xe", [2 * CAPP, DM], BF16).ap()
    agxf_out = nc.dram_tensor("agxf", [8 * TLOC, DM], BF16, addr_space="Shared").ap()
    agy_in = nc.dram_tensor("agy_in", [NSLOTP, DM], BF16).ap()
    agy_o = [nc.dram_tensor(f"agy_out{i}", [8 * CAPQ, DM], BF16,
                            addr_space="Shared").ap() for i in range(2)]

    c_ones128 = nc.inline_tensor(np.ones((128, 128), dtype=bfnp), name="c_ones128").ap()
    c_or64 = nc.inline_tensor(np.ones((1, 64), dtype=np.float32), name="c_or64").ap()
    c_or128 = nc.inline_tensor(np.ones((1, 128), dtype=np.float32), name="c_or128").ap()
    c_or128b = nc.inline_tensor(np.ones((1, 128), dtype=bfnp), name="c_or128b").ap()
    c_id64 = nc.inline_tensor(np.vstack([np.eye(64, dtype=bfnp)] * 2), name="c_id64").ap()
    c_id32 = nc.inline_tensor(np.eye(32, dtype=np.float32), name="c_id32").ap()
    c_triu64 = nc.inline_tensor(
        np.vstack([np.triu(np.ones((64, 64), dtype=np.float32))] * 2), name="c_triu64").ap()
    c_tri128 = nc.inline_tensor(
        np.triu(np.ones((128, 128), dtype=bfnp), k=1), name="c_tri128").ap()
    c_triNT = nc.inline_tensor(
        np.triu(np.ones((NTT, NTT), dtype=np.float32), k=1), name="c_triNT").ap()
    c_ocNT = nc.inline_tensor(np.ones((NTT, 1), dtype=np.float32), name="c_ocNT").ap()
    c_erow = nc.inline_tensor(
        np.tile((np.arange(NE) * CAPQ).astype(np.float32), (128, 1)),
        name="c_erow").ap()
    kb = np.zeros((128, 2), dtype=np.float32)
    kb[:, 0] = EPS
    kb[:, 1] = 1.0
    c_kbias = nc.inline_tensor(kb, name="c_kbias").ap()

    with ExitStack() as stack, nc.Block() as block:
        b = Bld(nc, block)
        b.setup(stack)

        pool = stack.enter_context(nc.psum_tensor("pool", [128, 4096], F32))
        BK = [Res(f"bk{i}") for i in range(8)]

        def pv(p0, pn, c0, cn):
            ap = pool[p0:p0 + pn, c0:c0 + cn]
            banks = tuple(BK[i] for i in range(c0 // 512, -(-(c0 + cn) // 512)))
            return ap, banks

        def sbt(st, name, shape, dtype, side=None):
            return TB(st.enter_context(nc.sbuf_tensor(name, shape, dtype, side=side)), name)

        cst = sbt(stack, "cst", [64, 16], F32, side="right")
        b.dma("sync", cst.ap(), cconst, writes=(cst,))
        dtb = sbt(stack, "dtb_s", [NH, 1], F32, side="right")
        b.dma("sync", dtb.ap(), dtb_d, writes=(dtb,))
        alog = sbt(stack, "alog_s", [NH, 1], F32, side="right")
        b.dma("sync", alog.ap(), Alog_d, writes=(alog,))
        convw = sbt(stack, "convw_s", [128, NXBC, 4], F32, side="right")
        b.dma("sync", convw.ap(), convw_d, writes=(convw,))
        convb = sbt(stack, "convb_s", [128, NXBC], F32, side="right")
        b.dma("sync", convb.ap(), convb_d, writes=(convb,))
        ones128 = sbt(stack, "ones128", [128, 128], BF16, side="right")
        b.dma("sync", ones128.ap(), c_ones128, writes=(ones128,))
        or64 = sbt(stack, "or64", [1, 64], F32, side="right")
        b.dma("sync", or64.ap(), c_or64, writes=(or64,))
        or128 = sbt(stack, "or128", [1, 128], F32, side="right")
        b.dma("sync", or128.ap(), c_or128, writes=(or128,))
        or128b = sbt(stack, "or128b", [1, 128], BF16, side="right")
        b.dma("sync", or128b.ap(), c_or128b, writes=(or128b,))
        id64 = sbt(stack, "id64", [128, 64], BF16, side="right")
        b.dma("sync", id64.ap(), c_id64, writes=(id64,))
        id32 = sbt(stack, "id32", [32, 32], F32, side="right")
        b.dma("sync", id32.ap(), c_id32, writes=(id32,))
        triu64 = sbt(stack, "triu64", [128, 64], F32, side="right")
        b.dma("sync", triu64.ap(), c_triu64, writes=(triu64,))
        tri128 = sbt(stack, "tri128", [128, 128], BF16, side="right")
        b.dma("sync", tri128.ap(), c_tri128, writes=(tri128,))
        triNT = sbt(stack, "triNT", [NTT, NTT], F32, side="right")
        b.dma("sync", triNT.ap(), c_triNT, writes=(triNT,))
        ocNT = sbt(stack, "ocNT", [NTT, 1], F32, side="right")
        b.dma("sync", ocNT.ap(), c_ocNT, writes=(ocNT,))
        kbias = sbt(stack, "kbias", [128, 2], F32, side="right")
        b.dma("sync", kbias.ap(), c_kbias, writes=(kbias,))

        A_col = sbt(stack, "A_col", [NH, 1], F32, side="right")
        b.op("scalar", lambda e: e.activation(A_col.ap(), alog.ap(), AF.Exp),
             reads=(alog,), writes=(A_col,))
        b.op("vector", lambda e: e.tensor_scalar_mul(A_col.ap(), A_col.ap(), -1.0),
             reads=(A_col,), writes=(A_col,))

        stP = ExitStack()
        dt_fm = sbt(stP, "dt_fm", [NH, TLOC], F32)
        cs_fm = sbt(stP, "cs_fm", [NH, TLOC], F32)
        B_fm = sbt(stP, "B_fm", [DST, TLOC], BF16)
        C_fm = sbt(stP, "C_fm", [DST, TLOC], BF16)
        csT = sbt(stP, "csT", [128, NTT, NH], F32)
        dtT = sbt(stP, "dtT", [128, NTT, NH], F32)

        spx_r = Res("sp_x")
        spz_r = Res("sp_z")
        spB_r = Res("sp_B")
        spcs_r = Res("sp_cs")
        sppd_r = Res("sp_pd")
        spyn_r = Res("sp_yn")
        spxf_r = Res("sp_xf")
        spwf_r = Res("sp_wf")
        tbl_r = Res("tbl")
        tmn_r = Res("tmn")
        agxf_r = Res("agxf")
        agyiR = [Res("agy_in0"), Res("agy_in1")]
        agyoR = [Res("agy_out0"), Res("agy_out1")]
        xeR = [Res("xe0"), Res("xe1")]
        sprt_r = Res("sp_rt")
        spro_r = Res("sp_ro")
        agi_r = Res("ag_in")
        ago_r = Res("ag_out")
        agci_r = Res("agc_in")
        agco_r = Res("agc_out")
        out_r = Res("out")

        DO_TILES = [(j, 128) for j in range(34)] + [(34, 32)]

        st1 = ExitStack()
        win = sbt(st1, "win", [128, 8, DPROJ], BF16)
        b.dma("gpsimd", win.ap(), WinT_d.rearrange("(k p) f -> p k f", p=128),
              writes=(win,))
        P_run = sbt(st1, "P_run", [NH, 1], F32)
        b.op("vector", lambda e: e.memset(P_run.ap(), 0.0), writes=(P_run,))
        pbe = sbt(st1, "pbe", [NH, 1], F32)
        That = sbt(st1, "That", [128, DIN], F32)
        ub = sbt(st1, "ub", [128, 8, 516], F32)
        m_sb = sbt(st1, "m_sb", [128, 516], F32)
        r_sb = sbt(st1, "r_sb", [128, 516], F32)
        xln = sbt(st1, "xln", [128, 8, 516], BF16)
        xsq = sbt(st1, "xsq", [128, 516], BF16)
        xb16 = sbt(st1, "xb16", [128, 516], BF16)
        xbc = [sbt(st1, f"xbc{i}", [128, 516], F32) for i in range(3)]
        cacc = sbt(st1, "cacc", [128, 512], F32)
        xstage = sbt(st1, "xstage", [128, 16, 512], BF16)
        zstage = sbt(st1, "zstage", [128, 16, 512], BF16)
        dtev = sbt(st1, "dtev", [NH, 512], F32)
        sp1 = sbt(st1, "sp1", [NH, 512], F32)
        sp2 = sbt(st1, "sp2", [NH, 512], F32)
        pd_blk = sbt(st1, "pd_blk", [NH, 9], F32)
        b.op("vector", lambda e: e.memset(pd_blk.ap(), 0.0), writes=(pd_blk,))
        pdflat = sbt(st1, "pdflat", [1, 9 * NH], F32)
        xT = [sbt(st1, f"xT{i}", [128, DIN], BF16) for i in range(2)]
        bT = [sbt(st1, f"bT{i}", [128, DST], BF16) for i in range(2)]
        scb = sbt(st1, "scb", [128, NH], BF16)
        scf = sbt(st1, "scf", [128, NH], F32)
        xds = sbt(st1, "xds", [128, DIN], BF16)
        resc = sbt(st1, "resc", [128, NH], F32)

        for kblk, (t0, W) in enumerate(blocks):
            WR = W - 4
            cpb = WR // CK
            b.dma("sync", ub[:, :, 0:W],
                  uT.rearrange("(k p) t -> p k t", p=128)[:, :, t0:t0 + W],
                  writes=(ub,))
            apA, bkA = pv(0, 128, 0, W)
            apB, bkB = pv(0, 128, 1024, W)
            apC, bkC = pv(0, 128, 2048, W)
            apD, bkD = pv(0, 128, 3072, W)
            wregs = [(apA, bkA), (apB, bkB), (apC, bkC), (apD, bkD)]
            for k in range(8):
                b.op("vector", lambda e, W=W, k=k: e.tensor_copy(xb16[:, 0:W], ub[:, k, 0:W]),
                     reads=(ub,), writes=(xb16,))
                for n0 in range(0, W, 512):
                    nw = min(512, W - n0)
                    b.mm(lambda e, apA=apA, k=k, n0=n0, nw=nw: e.matmul(
                        apA[:, n0:n0 + nw], ones128.ap(), xb16[:, n0:n0 + nw],
                        start=(k == 0), stop=(k == 7)),
                        reads=(xb16, ones128), writes=bkA)
                b.op("scalar", lambda e, W=W, k=k: e.activation(
                    xsq[:, 0:W], ub[:, k, 0:W], AF.Square),
                    reads=(ub,), writes=(xsq,))
                for n0 in range(0, W, 512):
                    nw = min(512, W - n0)
                    b.mm(lambda e, apB=apB, k=k, n0=n0, nw=nw: e.matmul(
                        apB[:, n0:n0 + nw], ones128.ap(), xsq[:, n0:n0 + nw],
                        start=(k == 0), stop=(k == 7)),
                        reads=(xsq, ones128), writes=bkB)
            b.op("vector", lambda e, W=W, apA=apA: e.tensor_scalar_mul(m_sb[:, 0:W], apA[:, 0:W], 1.0 / DM),
                 reads=bkA, writes=(m_sb,))
            b.op("vector", lambda e, W=W: e.scalar_tensor_tensor(
                r_sb[:, 0:W], m_sb[:, 0:W], -1.0, m_sb[:, 0:W], ALU.mult, ALU.mult),
                reads=(m_sb,), writes=(r_sb,))
            b.op("vector", lambda e, W=W, apB=apB: e.scalar_tensor_tensor(
                r_sb[:, 0:W], apB[:, 0:W], 1.0 / DM, r_sb[:, 0:W], ALU.mult, ALU.add),
                reads=bkB + (r_sb,), writes=(r_sb,))
            b.op("scalar", lambda e, W=W: e.activation(r_sb[:, 0:W], r_sb[:, 0:W], AF.Ln,
                                                  bias=kbias.t[:, 0:1]),
                 reads=(r_sb, kbias), writes=(r_sb,))
            b.op("scalar", lambda e, W=W: e.activation(r_sb[:, 0:W], r_sb[:, 0:W], AF.Exp, scale=-0.5),
                 reads=(r_sb,), writes=(r_sb,))
            for k in range(8):
                b.op("vector", lambda e, W=W, k=k: e.tensor_sub(
                    xln[:, k, 0:W], ub[:, k, 0:W], m_sb[:, 0:W]),
                    reads=(ub, m_sb), writes=(xln,))
                b.op("vector", lambda e, W=W, k=k: e.tensor_mul(
                    xln[:, k, 0:W], xln[:, k, 0:W], r_sb[:, 0:W]),
                    reads=(xln, r_sb), writes=(xln,))

            for jt, nr in DO_TILES:
                app, bks = wregs[jt % 4]
                fns = []
                for k in range(8):
                    for n0 in range(0, W, 512):
                        nw = min(512, W - n0)
                        fns.append(lambda e, jt=jt, k=k, n0=n0, nw=nw, nr=nr, app=app:
                                   e.matmul(app[0:nr, n0:n0 + nw],
                                            win[:, k, jt * 128: jt * 128 + nr],
                                            xln[:, k, n0:n0 + nw],
                                            start=(k == 0), stop=(k == 7)))
                b.mm(fns, reads=(win, xln), writes=bks)
                if jt < 16:
                    b.op("scalar", lambda e, W=W, WR=WR, jt=jt, app=app: e.activation(
                        zstage[:, jt, 0:WR], app[:, 4:W], AF.Silu),
                        reads=bks, writes=(zstage,))
                elif jt < 34:
                    jx = jt - 16
                    xb = xbc[jx % 3]
                    b.op("scalar", lambda e, W=W, app=app, xb=xb: e.copy(
                        xb[:, 0:W], app[:, 0:W]), reads=bks, writes=(xb,))
                    b.op("vector", lambda e, WR=WR, jx=jx, xb=xb: e.tensor_scalar(
                        cacc[:, 0:WR], xb[:, 1:1 + WR], convw.t[:, jx, 0:1],
                        convb.t[:, jx:jx + 1], ALU.mult, ALU.add),
                        reads=(xb, convw, convb), writes=(cacc,))
                    for kk in range(1, 4):
                        b.op("vector", lambda e, WR=WR, jx=jx, kk=kk, xb=xb: e.scalar_tensor_tensor(
                            cacc[:, 0:WR], xb[:, 1 + kk:1 + kk + WR], convw.t[:, jx, kk:kk + 1],
                            cacc[:, 0:WR], ALU.mult, ALU.add),
                            reads=(xb, convw, cacc), writes=(cacc,))
                    if jx < 16:
                        b.op("scalar", lambda e, WR=WR, jx=jx: e.activation(
                            xstage[:, jx, 0:WR], cacc[:, 0:WR], AF.Silu),
                            reads=(cacc,), writes=(xstage,))
                    elif jx == 16:
                        b.op("scalar", lambda e, WR=WR, t0=t0: e.activation(
                            B_fm.t[:, t0:t0 + WR], cacc[:, 0:WR], AF.Silu),
                            reads=(cacc,), writes=(B_fm,))
                    else:
                        b.op("scalar", lambda e, WR=WR, t0=t0: e.activation(
                            C_fm.t[:, t0:t0 + WR], cacc[:, 0:WR], AF.Silu),
                            reads=(cacc,), writes=(C_fm,))
                else:
                    b.op("vector", lambda e, W=W, WR=WR, app=app: e.tensor_scalar_add(
                        dtev[:, 0:WR], app[0:NH, 4:W], dtb.t[:, 0:1]),
                        reads=bks + (dtb,), writes=(dtev,))

            b.op("scalar", lambda e, WR=WR: e.activation(sp1[:, 0:WR], dtev[:, 0:WR], AF.Abs),
                 reads=(dtev,), writes=(sp1,))
            b.op("scalar", lambda e, WR=WR: e.activation(sp1[:, 0:WR], sp1[:, 0:WR], AF.Exp, scale=-1.0),
                 reads=(sp1,), writes=(sp1,))
            b.op("scalar", lambda e, WR=WR: e.activation(sp1[:, 0:WR], sp1[:, 0:WR], AF.Ln,
                                                  bias=kbias.t[0:NH, 1:2]),
                 reads=(sp1, kbias), writes=(sp1,))
            b.op("vector", lambda e, WR=WR: e.tensor_scalar_max(sp2[:, 0:WR], dtev[:, 0:WR], 0.0),
                 reads=(dtev,), writes=(sp2,))
            b.op("vector", lambda e, WR=WR, t0=t0: e.tensor_add(
                dt_fm.t[:, t0:t0 + WR], sp1[:, 0:WR], sp2[:, 0:WR]),
                reads=(sp1, sp2), writes=(dt_fm,))

            b.dma("scalar", sp_x.rearrange("(j p) t -> p j t", p=128)[:, :, t0:t0 + WR],
                  xstage[:, :, 0:WR], reads=(xstage,), writes=(spx_r,))
            b.dma("scalar", sp_z.rearrange("(j p) t -> p j t", p=128)[:, :, t0:t0 + WR],
                  zstage[:, :, 0:WR], reads=(zstage,), writes=(spz_r,))
            b.dma("scalar", sp_B[:, t0:t0 + WR], B_fm.t[:, t0:t0 + WR],
                  reads=(B_fm,), writes=(spB_r,))

            b.op("vector", lambda e, WR=WR, t0=t0: e.tensor_scalar_mul(
                dtev[:, 0:WR], dt_fm.t[:, t0:t0 + WR], A_col.t[:, 0:1]),
                reads=(dt_fm, A_col), writes=(dtev,))
            b.op("vector", lambda e: e.memset(sp2[:, 0:CK], 0.0), writes=(sp2,))
            for c in range(cpb):
                b.op("vector", lambda e, t0=t0, c=c: e.tensor_tensor_scan(
                    cs_fm.t[:, t0 + c * CK: t0 + (c + 1) * CK],
                    dtev[:, c * CK:(c + 1) * CK], sp2[:, 0:CK], 0.0,
                    ALU.add, ALU.add),
                    reads=(dtev, sp2), writes=(cs_fm,))
            avw = bass.AP(cs_fm.t, t0 + CK - 1, [[TLOC, NH], [CK, cpb]])
            b.op("vector", lambda e, avw=avw, cpb=cpb: e.tensor_tensor_scan(
                pd_blk[:, 0:cpb], avw, sp2[:, 0:cpb], P_run.t[:, 0:1],
                ALU.add, ALU.add),
                reads=(cs_fm, sp2, P_run), writes=(pd_blk,))
            b.op("vector", lambda e, cpb=cpb: e.tensor_copy(pbe.ap(), pd_blk[:, cpb - 1:cpb]),
                 reads=(pd_blk,), writes=(pbe,))
            b.op("vector", lambda e: e.tensor_sub(pd_blk[:, 8:9], P_run.ap(), pbe.ap()),
                 reads=(P_run, pbe), writes=(pd_blk,))
            b.op("vector", lambda e: e.tensor_copy(P_run.ap(), pbe.ap()),
                 reads=(pbe,), writes=(P_run,))
            b.op("vector", lambda e, avw=avw, cpb=cpb: e.tensor_sub(pd_blk[:, 0:cpb], pd_blk[:, 0:cpb], avw),
                 reads=(pd_blk, cs_fm), writes=(pd_blk,))
            b.op("vector", lambda e, cpb=cpb: e.tensor_scalar(
                pd_blk[:, 0:cpb], pd_blk[:, 0:cpb], pbe.t[:, 0:1], None, ALU.subtract),
                reads=(pd_blk, pbe), writes=(pd_blk,))

            b.dma("scalar", sp_cs[:, t0:t0 + WR], cs_fm.t[:, t0:t0 + WR],
                  reads=(cs_fm,), writes=(spcs_r,))
            b.dma("scalar", sp_pd[:, 9 * kblk: 9 * (kblk + 1)], pd_blk.ap(),
                  reads=(pd_blk,), writes=(sppd_r,))
            b.dma("scalar", pdflat.ap().rearrange("o (h c) -> o h c", h=NH),
                  bass.AP(sp_pd.tensor, 9 * kblk, [[1, 1], [9 * NBLK, NH], [1, 9]]),
                  reads=(sppd_r,), writes=(pdflat,))

            tt0 = t0 // 128
            apT, bkT = pv(0, 128, 2048, 64)
            for tt in range(tt0, tt0 + (0 if NO_TR else WR // 128)):
                b.mm(lambda e, apT=apT, tt=tt: e.transpose(
                    apT[:, 0:NH], cs_fm.t[:, tt * 128:(tt + 1) * 128], id32.ap()),
                    reads=(cs_fm, id32), writes=bkT)
                b.op("vector", lambda e, apT=apT, tt=tt: e.tensor_copy(
                    csT.t[:, tt, :], apT[:, 0:NH]), reads=bkT, writes=(csT,))
                b.mm(lambda e, apT=apT, tt=tt: e.transpose(
                    apT[:, 0:NH], dt_fm.t[:, tt * 128:(tt + 1) * 128], id32.ap()),
                    reads=(dt_fm, id32), writes=bkT)
                b.op("vector", lambda e, apT=apT, tt=tt: e.tensor_copy(
                    dtT.t[:, tt, :], apT[:, 0:NH]), reads=bkT, writes=(dtT,))

            apU, bkU = pv(0, 128, 0, 2048)
            apU2, bkU2 = pv(0, 128, 2048, 2048)
            apb_, bkb_ = pv(0, 128, 2048 + 1024, 64)
            for pi in range(0 if NO_U else cpb // 2):
                tt = (t0 // 128) + pi
                xTt = xT[pi % 2]
                bTt = bT[pi % 2]
                b.dma("sync", xTt.ap(), sp_x[:, tt * 128:(tt + 1) * 128],
                      reads=(spx_r,), writes=(xTt,), transpose=True)
                b.dma("sync", bTt.ap(), sp_B[:, tt * 128:(tt + 1) * 128],
                      reads=(spB_r,), writes=(bTt,), transpose=True)
                b.mm([lambda e, apb_=apb_, ci=2 * pi + hh: e.matmul(
                    apb_[64 * (ci % 2):64 * (ci % 2) + 64, 0:NH], or64.ap(),
                    bass.AP(pdflat.t, ci, [[9 * NH, 1], [9, NH]]),
                    start=True, stop=True) for hh in range(2)],
                    reads=(pdflat, or64), writes=bkb_)
                b.op("vector", lambda e, apb_=apb_, tt=tt: e.tensor_add(
                    scf.ap(), csT.t[:, tt, :], apb_[:, 0:NH]),
                    reads=(csT,) + bkb_, writes=(scf,))
                b.op("scalar", lambda e: e.activation(scf.ap(), scf.ap(), AF.Exp, scale=-1.0),
                     reads=(scf,), writes=(scf,))
                b.op("vector", lambda e, tt=tt: e.tensor_mul(
                    scb.ap(), scf.ap(), dtT.t[:, tt, :]),
                    reads=(scf, dtT), writes=(scb,))
                scv = hview(scb[:, :], 128, HD)
                b.op("vector", lambda e, scv=scv, xTt=xTt: e.tensor_tensor(
                    xds.ap().rearrange("p (h d) -> p h d", h=NH),
                    xTt.ap().rearrange("p (h d) -> p h d", h=NH),
                    scv, ALU.mult),
                    reads=(xTt, scb), writes=(xds,))
                fns = []
                for hh in range(2):
                    ci = 2 * pi + hh
                    apx = apU if hh == 0 else apU2
                    for n0 in range(0, DIN, 512):
                        fns.append(lambda e, apx=apx, n0=n0, ci=ci, hh=hh, bTt=bTt, cpb=cpb:
                                   e.matmul(apx[:, n0:n0 + 512],
                                            bTt[64 * hh:64 * hh + 64, :],
                                            xds.t[64 * hh:64 * hh + 64, n0:n0 + 512],
                                            start=(ci < 2), stop=(ci >= cpb - 2)))
                b.mm(fns, reads=(bTt, xds), writes=bkU + bkU2)

            apb2, bkb2 = pv(0, 128, 1024, 64)
            if NO_U:
                pass
            elif kblk == 0:
                b.op("vector", lambda e, apU=apU: e.tensor_copy(That.ap(), apU[:, 0:DIN]),
                     reads=bkU, writes=(That,))
            else:
                b.mm(lambda e, apb2=apb2: e.matmul(
                    apb2[:, 0:NH], or128.ap(),
                    bass.AP(pdflat.t, 8, [[9 * NH, 1], [9, NH]]),
                    start=True, stop=True),
                    reads=(pdflat, or128), writes=bkb2)
                b.op("scalar", lambda e, apb2=apb2: e.activation(
                    resc.ap(), apb2[:, 0:NH], AF.Exp, scale=-1.0),
                    reads=bkb2, writes=(resc,))
                rv = hview(resc[:, :], 128, HD)
                b.op("vector", lambda e, rv=rv: e.tensor_tensor(
                    That.ap().rearrange("p (h d) -> p h d", h=NH),
                    That.ap().rearrange("p (h d) -> p h d", h=NH), rv, ALU.mult),
                    reads=(That, resc), writes=(That,))
                b.op("vector", lambda e, apU=apU: e.tensor_add(That.ap(), That.ap(), apU[:, 0:DIN]),
                     reads=(That,) + bkU, writes=(That,))

        b.dma("scalar", ag_in[0:128, :], That.ap(), reads=(That,), writes=(agi_r,))
        b.dma("scalar", ag_in[128:129, 0:NH], P_run.ap(), reads=(P_run,), writes=(agi_r,))
        b.op("vector", lambda e: e.memset(resc.ap(), 0.0), writes=(resc,))
        b.dma("scalar", ag_in[128:129, NH:DIN],
              bass.AP(resc.t, 0, [[NH, 1], [0, (DIN - NH) // NH], [1, NH]]),
              reads=(resc,), writes=(agi_r,))
        if not NO_AG:
            b.cc("AllGather", ALU.bypass, [[0, 1, 2, 3, 4, 5, 6, 7]],
                 ag_in, ag_out, reads=(agi_r,), writes=(ago_r,))
        b.barrier()
        st1.close()
        x1 = sbt(stack, "x1", [128, NTT, DM], F32, side="right")
        gtb = sbt(stack, "gtb", [128, NTT], F32, side="right")
        b.dma("sync", gtb.ap(), gtok_d, writes=(gtb,))
        erow = sbt(stack, "erow", [128, NE], F32, side="right")
        b.dma("sync", erow.ap(), c_erow, writes=(erow,))
        sidxw = sbt(stack, "sidxw", [128, 26], F32, side="right")
        b.dma("sync", sidxw.ap(), sidx_d, writes=(sidxw,))
        sidxi = sbt(stack, "sidxi", [128, 26], I32, side="right")
        b.op("vector", lambda e: e.tensor_copy(sidxi.ap(), sidxw.ap()),
             reads=(sidxw,), writes=(sidxi,))
        idxg = sbt(stack, "idxg", [128, NTT, 2], I32, side="right")

        if PHASE_LIMIT < 2:
            b.dma("sync", out_d.rearrange("(a p) f -> p a f", p=128), x1.ap(),
                  reads=(x1,), writes=(out_r,))
            b.barrier()
            b.flush()
            return nc
        st2 = ExitStack()
        st2a = ExitStack()
        T = sbt(st2, "T", [128, DIN], F32)
        atot = sbt(st2a, "atot", [8, NH], F32)
        b.dma("sync", atot.ap(),
              bass.AP(ag_out.tensor, 128 * DIN, [[129 * DIN, 8], [1, NH]]),
              reads=(ago_r,), writes=(atot,))
        apw, bkw = pv(0, 8, 0, 32)
        b.mm(lambda e: e.matmul(apw[:, 0:NH], cst.t[0:8, 2:10], atot.ap(),
                                start=True, stop=True),
             reads=(cst, atot), writes=bkw)
        wall = sbt(st2a, "wall", [8, NH], F32)
        b.op("scalar", lambda e: e.activation(wall.ap(), apw[:, 0:NH], AF.Exp),
             reads=bkw, writes=(wall,))
        b.op("vector", lambda e: e.tensor_scalar_mul(wall.ap(), wall.ap(), cst.t[0:8, 0:1]),
             reads=(wall, cst), writes=(wall,))
        b.dma("scalar", sp_wf[:, :], wall.ap(), reads=(wall,), writes=(spwf_r,))
        wfl = sbt(st2a, "wfl", [1, 8 * NH], F32)
        b.dma("scalar", wfl.ap(), sp_wf.rearrange("(o j) h -> o (j h)", o=1),
              reads=(spwf_r,), writes=(wfl,))
        apwb, bkwb = pv(0, 128, 0, 256)
        b.mm(lambda e: e.matmul(apwb[:, 0:256], or128.ap(), wfl.ap(),
                                start=True, stop=True),
             reads=(or128, wfl), writes=bkwb)
        wb = sbt(st2a, "wb", [128, 8 * NH], F32)
        b.op("scalar", lambda e: e.copy(wb.ap(), apwb[:, 0:256]),
             reads=bkwb, writes=(wb,))

        Sj = [sbt(st2a, "Sj0", [128, DIN], F32)] * 2
        apTmp, bkTmp = pv(0, 128, 0, 2048)
        for j in range(8):
            Sjt = Sj[j % 2]
            b.dma("sync", Sjt.ap(), ag_out[j * 129:j * 129 + 128, :],
                  reads=(ago_r,), writes=(Sjt,))
            wv = hview(wb[:, j * NH:(j + 1) * NH], 128, HD)
            if j == 0:
                b.op("vector", lambda e, Sjt=Sjt, wv=wv: e.tensor_tensor(
                    T.ap().rearrange("p (h d) -> p h d", h=NH),
                    Sjt.ap().rearrange("p (h d) -> p h d", h=NH), wv, ALU.mult),
                    reads=(Sjt, wb), writes=(T,))
            else:
                b.op("vector", lambda e, Sjt=Sjt, wv=wv: e.tensor_tensor(
                    apTmp[:, 0:DIN].rearrange("p (h d) -> p h d", h=NH),
                    Sjt.ap().rearrange("p (h d) -> p h d", h=NH), wv, ALU.mult),
                    reads=(Sjt, wb), writes=bkTmp)
                b.op("vector", lambda e: e.tensor_add(T.ap(), T.ap(), apTmp[:, 0:DIN]),
                     reads=(T,) + bkTmp, writes=(T,))

        b.barrier()
        st2a.close()
        ecs = sbt(st2, "ecs", [128, NTT, NH], F32)
        b.op("scalar", lambda e: e.activation(
            ecs.ap().rearrange("p a h -> p (a h)"),
            csT.ap().rearrange("p a h -> p (a h)"), AF.Exp),
            reads=(csT,), writes=(ecs,))

        dbr = sbt(st2, "dbr", [1, DIN], BF16)
        b.dma("gpsimd", dbr.ap(), Dhp_d, writes=(dbr,))
        apD, bkD = pv(0, 128, 0, 2048)
        b.mm([lambda e, n0=n0: e.matmul(apD[:, n0:n0 + 512], or128b.ap(),
                                        dbr.t[:, n0:n0 + 512], start=True, stop=True)
              for n0 in range(0, DIN, 512)],
             reads=(or128b, dbr), writes=bkD)
        Db = sbt(st2, "Db", [128, DIN], BF16)
        b.op("scalar", lambda e: e.copy(Db.ap(), apD[:, 0:DIN]), reads=bkD, writes=(Db,))

        TzB = [sbt(st2, f"TzB{i}", [128, DIN], BF16) for i in range(2)]
        zT = [sbt(st2, f"zT{i}", [128, DIN], BF16) for i in range(2)]
        xT2 = [sbt(st2, f"xT2{i}", [128, DIN], BF16) for i in range(2)]
        bT2 = [sbt(st2, f"bT2{i}", [128, DST], BF16) for i in range(2)]
        xdt = [sbt(st2, f"xdt{i}", [128, DIN], BF16) for i in range(2)]
        xDt = [sbt(st2, "xDt0", [128, DIN], BF16)] * 2
        xds2 = [sbt(st2, f"xds2{i}", [128, DIN], BF16) for i in range(2)]
        csflc = sbt(st2, "csflc", [1, 2 * NH * CK], F32)
        cstl2 = [sbt(st2, f"cstl{i}", [1, 2 * NH], F32) for i in range(2)]
        dzb2 = sbt(st2, "dzb2", [128, 2 * NH], F32)
        scf2 = sbt(st2, "scf2", [128, NH], F32)
        segE = sbt(st2, "segE", [128, DIN], F32)
        MT = [sbt(st2, "MT0", [128, DIN], BF16)] * 2
        GMT = sbt(st2, "GMT", [128, 64], BF16)
        ygf = sbt(st2, "ygf", [128, DIN], F32)
        ynb = [sbt(st2, f"ynb{i}", [128, DIN], BF16) for i in range(2)]
        rss = sbt(st2, "rss", [128, 1], F32)
        rr_ = sbt(st2, "rr_", [128, 1], F32)
        dz01 = sbt(st2, "dz01", [128, NH], F32)
        Wsc = sbt(st2, "Wsc", [128, DIN], BF16)

        for tt in range(NTT):
            par = tt % 2
            cstl = cstl2[par]
            xTt = xT2[par]
            zTt = zT[par]
            bTt = bT2[par]
            xdtt = xdt[par]
            xDtt = xDt[par]
            xdst = xds2[par]
            MTt = MT[par]
            ynbt = ynb[par]
            zA = tt * 2
            b.dma("sync", xTt.ap(), sp_x[:, tt * 128:(tt + 1) * 128],
                  reads=(spx_r,), writes=(xTt,), transpose=True)
            b.dma("sync", zTt.ap(), sp_z[:, tt * 128:(tt + 1) * 128],
                  reads=(spz_r,), writes=(zTt,), transpose=True)
            b.dma("sync", bTt.ap(), sp_B[:, tt * 128:(tt + 1) * 128],
                  reads=(spB_r,), writes=(bTt,), transpose=True)
            b.dma("gpsimd", csflc.ap().rearrange("o (c h l) -> o c h l", c=2, h=NH),
                  bass.AP(sp_cs.tensor, zA * CK,
                          [[1, 1], [CK, 2], [TLOC, NH], [1, CK]]),
                  reads=(spcs_r,), writes=(csflc,))
            b.dma("gpsimd", cstl.ap().rearrange("o (c h) -> o c h", c=2),
                  bass.AP(sp_cs.tensor, zA * CK + CK - 1,
                          [[1, 1], [CK, 2], [TLOC, NH]]),
                  reads=(spcs_r,), writes=(cstl,), slow=True)
            dtv = hview(dtT[:, tt, :], 128, HD)
            b.op("vector", lambda e, xTt=xTt, xdtt=xdtt, dtv=dtv: e.tensor_tensor(
                xdtt.ap().rearrange("p (h d) -> p h d", h=NH),
                xTt.ap().rearrange("p (h d) -> p h d", h=NH), dtv, ALU.mult),
                reads=(xTt, dtT), writes=(xdtt,))
            b.op("vector", lambda e, xTt=xTt, xDtt=xDtt: e.tensor_tensor(
                xDtt.ap(), xTt.ap(), Db.ap(), ALU.mult),
                reads=(xTt, Db), writes=(xDtt,))
            apd, bkd = pv(0, 128, 0, 64)
            b.mm([lambda e, apd=apd, hh=hh, cstl=cstl: e.matmul(
                apd[:, 32 * hh:32 * hh + NH], or128.ap(),
                bass.AP(cstl.t, hh * NH, [[2 * NH, 1], [1, NH]]),
                start=True, stop=True) for hh in range(2)],
                reads=(or128, cstl), writes=bkd)
            b.op("scalar", lambda e, apd=apd: e.activation(
                dzb2.ap(), apd[:, 0:2 * NH], AF.Exp), reads=bkd, writes=(dzb2,))
            apa, bka = pv(0, 128, 64, 32)
            b.mm([lambda e, apa=apa, hh=hh, cstl=cstl: e.matmul(
                apa[64 * hh:64 * hh + 64, 0:NH], or64.ap(),
                bass.AP(cstl.t, hh * NH, [[2 * NH, 1], [1, NH]]),
                start=True, stop=True) for hh in range(2)],
                reads=(or64, cstl), writes=bka)
            b.op("vector", lambda e, apa=apa, tt=tt: e.tensor_sub(
                scf2.ap(), csT.t[:, tt, :], apa[:, 0:NH]),
                reads=(csT,) + bka, writes=(scf2,))
            b.op("scalar", lambda e: e.activation(
                scf2.ap(), scf2.ap(), AF.Exp, scale=-1.0),
                reads=(scf2,), writes=(scf2,))
            scv = hview(scf2[:, :], 128, HD)
            b.op("vector", lambda e, scv=scv, xdtt=xdtt, xdst=xdst: e.tensor_tensor(
                xdst.ap().rearrange("p (h d) -> p h d", h=NH),
                xdtt.ap().rearrange("p (h d) -> p h d", h=NH),
                scv, ALU.mult),
                reads=(xdtt, scf2), writes=(xdst,))
            apUU, bkUU = pv(0, 128, 2048, 2048)
            dzv0 = dzb2.t[:, 0:NH].unsqueeze(2).broadcast_to([128, NH, HD])
            dzv1 = dzb2.t[:, NH:2 * NH].unsqueeze(2).broadcast_to([128, NH, HD])
            dz01v = dz01.t[:, :].unsqueeze(2).broadcast_to([128, NH, HD])
            b.mm([lambda e, apUU=apUU, n0=n0, bTt=bTt, xdst=xdst: e.matmul(
                apUU[:, n0:n0 + 512], bTt[0:64, :],
                xdst.t[0:64, n0:n0 + 512],
                start=True, stop=True)
                for n0 in range(0, DIN, 512)],
                reads=(bTt, xdst), writes=bkUU)
            b.op("vector", lambda e: e.tensor_mul(
                dz01.ap(), dzb2.t[:, 0:NH], dzb2.t[:, NH:2 * NH]),
                reads=(dzb2,), writes=(dz01,))
            b.op("scalar", lambda e: e.copy(TzB[0].ap(), T.ap()),
                 reads=(T,), writes=(TzB[0],))
            b.op("vector", lambda e, dzv0=dzv0: e.tensor_tensor(
                TzB[1].ap().rearrange("p (h d) -> p h d", h=NH),
                T.ap().rearrange("p (h d) -> p h d", h=NH), dzv0, ALU.mult),
                reads=(T, dzb2), writes=(TzB[1],))
            b.op("vector", lambda e, apUU=apUU: e.tensor_add(
                TzB[1].ap(), TzB[1].ap(), apUU[:, 0:DIN]),
                reads=(TzB[1],) + bkUU, writes=(TzB[1],))
            b.op("vector", lambda e, apUU=apUU, dzv1=dzv1: e.tensor_tensor(
                Wsc.ap().rearrange("p (h d) -> p h d", h=NH),
                apUU[:, 0:DIN].rearrange("p (h d) -> p h d", h=NH),
                dzv1, ALU.mult),
                reads=(dzb2,) + bkUU, writes=(Wsc,))
            b.mm([lambda e, apUU=apUU, n0=n0, bTt=bTt, xdst=xdst: e.matmul(
                apUU[:, n0:n0 + 512], bTt[64:128, :],
                xdst.t[64:128, n0:n0 + 512],
                start=True, stop=True)
                for n0 in range(0, DIN, 512)],
                reads=(bTt, xdst), writes=bkUU)
            b.op("vector", lambda e, dz01v=dz01v: e.tensor_tensor(
                T.ap().rearrange("p (h d) -> p h d", h=NH),
                T.ap().rearrange("p (h d) -> p h d", h=NH), dz01v, ALU.mult),
                reads=(T, dz01), writes=(T,))
            b.op("vector", lambda e: e.tensor_add(T.ap(), T.ap(), Wsc.ap()),
                 reads=(T, Wsc), writes=(T,))
            b.op("vector", lambda e, apUU=apUU: e.tensor_add(
                T.ap(), T.ap(), apUU[:, 0:DIN]),
                reads=(T,) + bkUU, writes=(T,))
            apTB, bkTB = pv(0, 128, 0, 2048)
            fns = []
            for hh in range(2):
                for h0 in (0, 8, 16, 24):
                    fns.append(lambda e, apTB=apTB, hh=hh, h0=h0, csflc=csflc: e.matmul(
                        apTB[64 * hh:64 * hh + 64, h0 * CK: h0 * CK + 512],
                        or64.ap(),
                        bass.AP(csflc.t, hh * NH * CK + h0 * CK,
                                [[2 * NH * CK, 1], [CK, 8], [1, CK]]),
                        start=True, stop=True))
            b.mm(fns, reads=(or64, csflc), writes=bkTB)
            csv = csT.t[:, tt, :].unsqueeze(2).broadcast_to([128, NH, CK])
            b.op("vector", lambda e, apTB=apTB, csv=csv: e.tensor_tensor(
                segE.ap().rearrange("p (h l) -> p h l", h=NH),
                apTB[:, 0:2048].rearrange("p (h l) -> p h l", h=NH),
                csv, ALU.subtract),
                reads=(csT,) + bkTB, writes=(segE,))
            b.op("vector", lambda e: e.tensor_single_scalar(
                segE.ap(), segE.ap(), 0.0, ALU.min),
                reads=(segE,), writes=(segE,))
            b.op("scalar", lambda e: e.activation(segE.ap(), segE.ap(), AF.Exp),
                 reads=(segE,), writes=(segE,))
            apG, bkG = pv(0, 128, 0, 64)
            b.mm([lambda e, apG=apG, hh=hh, zz=zA + hh: e.matmul(
                apG[64 * hh:64 * hh + 64, 0:64], B_fm.t[:, zz * CK:(zz + 1) * CK],
                C_fm.t[:, zz * CK:(zz + 1) * CK], start=True, stop=True)
                for hh in range(2)],
                reads=(B_fm, C_fm), writes=bkG)
            b.op("vector", lambda e, apG=apG: e.tensor_tensor(
                GMT.ap(), apG[:, 0:64], triu64.ap(), ALU.mult),
                reads=bkG + (triu64,), writes=(GMT,))
            gmv = GMT.t[:, :].unsqueeze(1).broadcast_to([128, NH, CK])
            b.op("vector", lambda e, gmv=gmv, MTt=MTt: e.tensor_tensor(
                MTt.ap().rearrange("p (h l) -> p h l", h=NH),
                segE.ap().rearrange("p (h l) -> p h l", h=NH),
                gmv, ALU.mult),
                reads=(segE, GMT), writes=(MTt,))
            apYB, bkYB = pv(0, 128, 2048, 2048)
            fns = []
            for hh in range(2):
                for n0 in range(0, DIN, 512):
                    fns.append(lambda e, apYB=apYB, hh=hh, n0=n0, xDtt=xDtt: e.matmul(
                        apYB[64 * hh:64 * hh + 64, n0:n0 + 512],
                        id64.t[64 * hh:64 * hh + 64, :],
                        xDtt[64 * hh:64 * hh + 64, n0:n0 + 512],
                        start=True, stop=False))
                for h in range(NH):
                    fns.append(lambda e, apYB=apYB, hh=hh, h=h, MTt=MTt, xdtt=xdtt: e.matmul(
                        apYB[64 * hh:64 * hh + 64, h * HD:(h + 1) * HD],
                        MTt.t[64 * hh:64 * hh + 64, h * HD:(h + 1) * HD],
                        xdtt.t[64 * hh:64 * hh + 64, h * HD:(h + 1) * HD],
                        start=False, stop=(h % 8 == 7)))
            b.mm(fns, reads=(id64, xDtt, MTt, xdtt), writes=bkYB)
            apYA, bkYA = pv(0, 128, 0, 2048)
            fns = []
            for hh in range(2):
                for n0 in range(0, DIN, 512):
                    fns.append(lambda e, apYA=apYA, hh=hh, n0=n0, zz=zA + hh: e.matmul(
                        apYA[64 * hh:64 * hh + 64, n0:n0 + 512],
                        C_fm.t[:, zz * CK:(zz + 1) * CK],
                        TzB[hh][:, n0:n0 + 512],
                        start=True, stop=True))
            b.mm(fns, reads=(C_fm, TzB[0], TzB[1]), writes=bkYA)
            ecv = ecs.t[:, tt, :].unsqueeze(2).broadcast_to([128, NH, HD])
            b.op("vector", lambda e, apYA=apYA, ecv=ecv: e.tensor_tensor(
                ygf.ap().rearrange("p (h d) -> p h d", h=NH),
                apYA[:, 0:2048].rearrange("p (h d) -> p h d", h=NH),
                ecv, ALU.mult),
                reads=(ecs,) + bkYA, writes=(ygf,))
            b.op("vector", lambda e, apYB=apYB: e.tensor_add(
                ygf.ap(), ygf.ap(), apYB[:, 0:2048]),
                reads=(ygf,) + bkYB, writes=(ygf,))
            b.op("vector", lambda e, zTt=zTt: e.tensor_mul(
                ygf.ap(), ygf.ap(), zTt.ap()),
                reads=(ygf, zTt), writes=(ygf,))
            b.op("scalar", lambda e, xdst=xdst: e.activation(
                xdst.ap(), ygf.ap(), AF.Square, accum_out=rss.ap()),
                reads=(ygf,), writes=(xdst, rss))
            b.op("vector", lambda e: e.tensor_scalar(
                rr_.ap(), rss.ap(), 1.0 / DIN, EPS, ALU.mult, ALU.add),
                reads=(rss,), writes=(rr_,))
            b.op("scalar", lambda e: e.activation(rr_.ap(), rr_.ap(), AF.Ln),
                 reads=(rr_,), writes=(rr_,))
            b.op("scalar", lambda e: e.activation(rr_.ap(), rr_.ap(), AF.Exp, scale=-0.5),
                 reads=(rr_,), writes=(rr_,))
            b.op("vector", lambda e, ynbt=ynbt: e.tensor_scalar_mul(
                ynbt.ap(), ygf.ap(), rr_.t[:, 0:1]),
                reads=(ygf, rr_), writes=(ynbt,))
            b.dma("scalar", sp_yn[tt * 128:(tt + 1) * 128, :], ynbt.ap(),
                  reads=(ynbt,), writes=(spyn_r,))
        b.barrier()
        st2.close()
        stP.close()

        if PHASE_LIMIT < 3:
            b.dma("sync", out_d.rearrange("(a p) f -> p a f", p=128), x1.ap(),
                  reads=(x1,), writes=(out_r,))
            b.barrier()
            b.flush()
            return nc
        st3 = ExitStack()
        woutb = sbt(st3, "woutb", [128, 16, DM], BF16)
        b.dma("gpsimd", woutb.ap(), WoutT_d.rearrange("(k p) f -> p k f", p=128),
              writes=(woutb,))
        nrm = sbt(st3, "nrm", [128, 16], F32)
        b.dma("sync", nrm.ap(), nrmw_d, writes=(nrm,))
        for k in range(16):
            b.op("vector", lambda e, k=k: e.tensor_scalar_mul(
                woutb.t[:, k, :], woutb.t[:, k, :], nrm.t[:, k:k + 1]),
                reads=(woutb, nrm), writes=(woutb,))
        ynf = sbt(st3, "ynf", [128, 16, TLOC], BF16)
        for di in range(16):
            b.dma("sync", ynf.t[:, di, :], sp_yn[:, di * 128:(di + 1) * 128],
                  reads=(spyn_r,), writes=(ynf,), transpose=True)
        ut = [sbt(st3, f"ut{i}", [128, DM], F32) for i in range(2)]
        m2 = sbt(st3, "m2", [128, 1], F32)
        v2 = sbt(st3, "v2", [128, 1], F32)
        s2 = sbt(st3, "s2", [128, 1], F32)
        dump2 = sbt(st3, "dump2", [128, DM], BF16)
        xfstage = sbt(st3, "xfstage", [128, DM], BF16)
        for tt in range(NTT):
            apY, bkY = pv(0, 128, 0, 1024)
            fns = []
            for di in range(16):
                for n0 in (0, 512):
                    fns.append(lambda e, apY=apY, di=di, n0=n0, tt=tt: e.matmul(
                        apY[:, n0:n0 + 512], ynf.t[:, di, tt * 128:(tt + 1) * 128],
                        woutb.t[:, di, n0:n0 + 512],
                        start=(di == 0), stop=(di == 15)))
            b.mm(fns, reads=(ynf, woutb), writes=bkY)
            utt = ut[tt % 2]
            b.dma("sync", utt.ap(), u_tok[tt * 128:(tt + 1) * 128, :], writes=(utt,))
            b.op("vector", lambda e, apY=apY, tt=tt, utt=utt: e.tensor_add(
                x1.t[:, tt, :], apY[:, 0:DM], utt.ap()),
                reads=bkY + (utt,), writes=(x1,))
            b.op("vector", lambda e, tt=tt: e.reduce_sum(
                m2.ap(), x1.t[:, tt, :], AX.X), reads=(x1,), writes=(m2,))
            b.op("vector", lambda e: e.tensor_scalar_mul(m2.ap(), m2.ap(), 1.0 / DM),
                 reads=(m2,), writes=(m2,))
            b.op("scalar", lambda e, tt=tt: e.activation(
                dump2.ap(), x1.t[:, tt, :], AF.Square, accum_out=s2.ap()),
                reads=(x1,), writes=(dump2, s2))
            b.op("vector", lambda e: e.tensor_mul(v2.ap(), m2.ap(), m2.ap()),
                 reads=(m2,), writes=(v2,))
            b.op("vector", lambda e: e.scalar_tensor_tensor(
                v2.ap(), s2.ap(), 1.0 / DM, v2.ap(), ALU.mult, ALU.subtract),
                reads=(s2, v2), writes=(v2,))
            b.op("scalar", lambda e: e.activation(v2.ap(), v2.ap(), AF.Ln,
                                                  bias=kbias.t[:, 0:1]),
                 reads=(v2, kbias), writes=(v2,))
            b.op("scalar", lambda e: e.activation(v2.ap(), v2.ap(), AF.Exp, scale=-0.5),
                 reads=(v2,), writes=(v2,))
            b.op("vector", lambda e, tt=tt: e.tensor_scalar(
                xfstage.ap(), x1.t[:, tt, :], m2.ap(), v2.ap(), ALU.subtract, ALU.mult),
                reads=(x1, m2, v2), writes=(xfstage,))
            b.dma("scalar", sp_xf[tt * 128:(tt + 1) * 128, :], xfstage.ap(),
                  reads=(xfstage,), writes=(spxf_r,))
        b.barrier()
        st3.close()

        if PHASE_LIMIT < 4:
            b.dma("sync", out_d.rearrange("(a p) f -> p a f", p=128), x1.ap(),
                  reads=(x1,), writes=(out_r,))
            b.barrier()
            b.flush()
            return nc
        st5 = ExitStack()
        wg = sbt(st5, "wg", [128, 8, DHID], BF16)
        wu = sbt(st5, "wu", [128, 8, DHID], BF16)
        wd = sbt(st5, "wd", [128, 16, DM], BF16)
        st4 = ExitStack()
        gate = sbt(st4, "gate", [128, 8, NE], BF16)
        b.dma("gpsimd", gate.ap(), gateT_d.rearrange("(k p) e -> p k e", p=128),
              writes=(gate,))
        wgv = WgT_d.rearrange("e d h -> (e d) h")
        wuv = WuT_d.rearrange("e d h -> (e d) h")
        wdv = WdT_d.rearrange("e h f -> (e h) f")
        for k in range(8):
            b.idma(wg.t[:, k, :], None, wgv, IOA(sidxi.t[:, 1 + k:2 + k], 0),
                   reads=(sidxi,), writes=(wg,))
            b.idma(wu.t[:, k, :], None, wuv, IOA(sidxi.t[:, 1 + k:2 + k], 0),
                   reads=(sidxi,), writes=(wu,))
        for dh in range(16):
            b.idma(wd.t[:, dh, :], None, wdv, IOA(sidxi.t[:, 9 + dh:10 + dh], 0),
                   reads=(sidxi,), writes=(wd,))
        xff = sbt(st4, "xff", [128, 8, TLOC], BF16)
        for k in range(8):
            b.dma("sync", xff.t[:, k, :], sp_xf[:, k * 128:(k + 1) * 128],
                  reads=(spxf_r,), writes=(xff,), transpose=True)
        TBE = NE * NSLOTP * 2
        ztb = sbt(st4, "ztb", [128, TBE // 128], F32)
        b.op("vector", lambda e: e.memset(ztb.ap(), 0.0), writes=(ztb,))
        b.dma("scalar", tbl_d.rearrange("a c -> (a c)")
              .rearrange("(p f) -> p f", p=128), ztb.ap(),
              reads=(ztb,), writes=(tbl_r,))
        b.cc("AllGather", ALU.bypass, [[0, 1, 2, 3, 4, 5, 6, 7]],
             sp_xf, agxf_out, reads=(spxf_r,), writes=(agxf_r,))
        probs = sbt(st4, "probs", [128, NE], F32)
        nmx = sbt(st4, "nmx", [128, 1], F32)
        sume = sbt(st4, "sume", [128, 1], F32)
        m1 = sbt(st4, "m1", [128, 1], F32)
        mm2 = sbt(st4, "mm2", [128, 1], F32)
        oh1 = sbt(st4, "oh1", [128, NE], F32)
        oh2 = sbt(st4, "oh2", [128, NE], F32)
        csx = sbt(st4, "csx", [128, NE], F32)
        zer = sbt(st4, "zer", [128, NE], F32)
        p2_ = sbt(st4, "p2_", [128, NE], F32)
        w12 = sbt(st4, "w12", [128, 2], F32)
        ohb = sbt(st4, "ohb", [128, NE], BF16)
        Pl = sbt(st4, "Pl", [128, NTT, 2, NE], F32)
        totf = sbt(st4, "totf", [1, 16 * NTT], F32)
        keepm = sbt(st4, "keepm", [128, NE], F32)
        comb = sbt(st4, "comb", [128, NTT, NE], F32)
        comb2 = sbt(st4, "comb2", [128, NTT, NE], F32)
        b.op("vector", lambda e: e.memset(zer.ap(), 0.0), writes=(zer,))

        for tt in range(NTT):
            apL, bkL = pv(0, 128, 0, 8)
            b.mm([lambda e, apL=apL, k=k, tt=tt: e.matmul(
                apL[:, 0:NE], xff.t[:, k, tt * 128:(tt + 1) * 128], gate.t[:, k, :],
                start=(k == 0), stop=(k == 7)) for k in range(8)],
                reads=(xff, gate), writes=bkL)
            b.op("vector", lambda e, apL=apL: e.reduce_max(nmx.ap(), apL[:, 0:NE], AX.X, negate=True),
                 reads=bkL, writes=(nmx,))
            b.op("scalar", lambda e, apL=apL: e.activation(
                probs.ap(), apL[:, 0:NE], AF.Exp, bias=nmx.ap(), accum_out=sume.ap()),
                reads=bkL + (nmx,), writes=(probs, sume))
            b.op("vector", lambda e: e.reciprocal(sume.ap(), sume.ap()),
                 reads=(sume,), writes=(sume,))
            b.op("vector", lambda e: e.tensor_scalar_mul(probs.ap(), probs.ap(), sume.ap()),
                 reads=(probs, sume), writes=(probs,))
            b.op("vector", lambda e: e.reduce_max(m1.ap(), probs.ap(), AX.X),
                 reads=(probs,), writes=(m1,))
            b.op("vector", lambda e: e.tensor_single_scalar(
                oh1.ap(), probs.ap(), m1.ap(), ALU.is_ge), reads=(probs, m1), writes=(oh1,))
            b.op("vector", lambda e: e.tensor_tensor_scan(
                csx.ap(), oh1.ap(), zer.ap(), 0.0, ALU.add, ALU.add),
                reads=(oh1, zer), writes=(csx,))
            b.op("vector", lambda e: e.tensor_single_scalar(
                csx.ap(), csx.ap(), 1.5, ALU.is_le), reads=(csx,), writes=(csx,))
            b.op("vector", lambda e: e.tensor_mul(oh1.ap(), oh1.ap(), csx.ap()),
                 reads=(oh1, csx), writes=(oh1,))
            b.op("vector", lambda e: e.tensor_scalar(
                csx.ap(), oh1.ap(), -1.0, 1.0, ALU.mult, ALU.add),
                reads=(oh1,), writes=(csx,))
            b.op("vector", lambda e: e.tensor_mul(p2_.ap(), probs.ap(), csx.ap()),
                 reads=(probs, csx), writes=(p2_,))
            b.op("vector", lambda e: e.reduce_max(mm2.ap(), p2_.ap(), AX.X),
                 reads=(p2_,), writes=(mm2,))
            b.op("vector", lambda e: e.tensor_single_scalar(
                oh2.ap(), p2_.ap(), mm2.ap(), ALU.is_ge), reads=(p2_, mm2), writes=(oh2,))
            b.op("vector", lambda e: e.tensor_tensor_scan(
                csx.ap(), oh2.ap(), zer.ap(), 0.0, ALU.add, ALU.add),
                reads=(oh2, zer), writes=(csx,))
            b.op("vector", lambda e: e.tensor_single_scalar(
                csx.ap(), csx.ap(), 1.5, ALU.is_le), reads=(csx,), writes=(csx,))
            b.op("vector", lambda e: e.tensor_mul(oh2.ap(), oh2.ap(), csx.ap()),
                 reads=(oh2, csx), writes=(oh2,))
            b.op("vector", lambda e: e.tensor_add(w12.t[:, 0:1], m1.ap(), mm2.ap()),
                 reads=(m1, mm2), writes=(w12,))
            b.op("vector", lambda e: e.tensor_scalar_add(
                w12.t[:, 0:1], w12.t[:, 0:1], 1e-9), reads=(w12,), writes=(w12,))
            b.op("vector", lambda e: e.reciprocal(w12.t[:, 1:2], w12.t[:, 0:1]),
                 reads=(w12,), writes=(w12,))
            b.op("vector", lambda e: e.tensor_mul(m1.ap(), m1.ap(), w12.t[:, 1:2]),
                 reads=(m1, w12), writes=(m1,))
            b.op("vector", lambda e: e.tensor_mul(mm2.ap(), mm2.ap(), w12.t[:, 1:2]),
                 reads=(mm2, w12), writes=(mm2,))
            for kk, oh in ((0, oh1), (1, oh2)):
                b.op("vector", lambda e, oh=oh: e.tensor_copy(ohb.ap(), oh.ap()),
                     reads=(oh,), writes=(ohb,))
                apP, bkP = pv(0, 128, 512, 8)
                b.mm(lambda e, apP=apP: e.matmul(apP[:, 0:NE], tri128.ap(), ohb.ap(),
                                        start=True, stop=True),
                     reads=(tri128, ohb), writes=bkP)
                b.op("vector", lambda e, apP=apP, tt=tt, kk=kk: e.tensor_copy(
                    Pl.t[:, tt, kk, :], apP[:, 0:NE]), reads=bkP, writes=(Pl,))
                apt, bkt = pv(0, 1, 1024, 8)
                b.mm(lambda e, apt=apt: e.matmul(apt[:, 0:NE], ones128[:, 0:1], ohb.ap(),
                                        start=True, stop=True),
                     reads=(ones128, ohb), writes=bkt)
                b.op("vector", lambda e, apt=apt, tt=tt, kk=kk: e.tensor_copy(
                    totf.t[:, tt * 16 + kk * 8: tt * 16 + kk * 8 + 8], apt[:, 0:NE]),
                    reads=bkt, writes=(totf,))
            b.op("vector", lambda e: e.tensor_scalar_mul(oh1.ap(), oh1.ap(), m1.ap()),
                 reads=(oh1, m1), writes=(oh1,))
            b.op("vector", lambda e: e.tensor_scalar_mul(oh2.ap(), oh2.ap(), mm2.ap()),
                 reads=(oh2, mm2), writes=(oh2,))
            b.op("vector", lambda e, tt=tt: e.tensor_copy(comb.t[:, tt, 0:NE], oh1.ap()),
                 reads=(oh1,), writes=(comb,))
            b.op("vector", lambda e, tt=tt: e.tensor_copy(comb2.t[:, tt, 0:NE], oh2.ap()),
                 reads=(oh2,), writes=(comb2,))

        b.dma("scalar", sp_rt[:, :], totf.ap(), reads=(totf,), writes=(sprt_r,))
        tot2 = sbt(st4, "tot2", [NTT, 16], F32)
        b.dma("scalar", tot2.ap(), sp_rt.rearrange("o (a b) -> (o a) b", b=16),
              reads=(sprt_r,), writes=(tot2,))
        apO, bkO = pv(0, NTT, 0, 16)
        b.mm(lambda e: e.matmul(apO[:, 0:16], triNT.ap(), tot2.ap(),
                                start=True, stop=True),
             reads=(triNT, tot2), writes=bkO)
        offs = sbt(st4, "offs", [NTT, 16], F32)
        b.op("vector", lambda e: e.tensor_copy(offs.ap(), apO[:, 0:16]),
             reads=bkO, writes=(offs,))
        apC, bkC = pv(0, 1, 512, 16)
        b.mm(lambda e: e.matmul(apC[:, 0:16], ocNT.ap(), tot2.ap(),
                                start=True, stop=True),
             reads=(ocNT, tot2), writes=bkC)
        ctot = sbt(st4, "ctot", [1, 16], F32)
        b.op("vector", lambda e: e.tensor_copy(ctot.ap(), apC[:, 0:16]),
             reads=bkC, writes=(ctot,))
        b.dma("scalar", agc_in[:, :], ctot.ap(), reads=(ctot,), writes=(agci_r,))
        b.cc("AllGather", ALU.bypass, [[0, 1, 2, 3, 4, 5, 6, 7]],
             agc_in, agc_out, reads=(agci_r,), writes=(agco_r,))
        cnt8 = sbt(st4, "cnt8", [8, 16], F32)
        b.dma("sync", cnt8.ap(), agc_out, reads=(agco_r,), writes=(cnt8,))
        apGO, bkGO = pv(0, 1, 1024, 16)
        b.mm(lambda e: e.matmul(apGO[:, 0:16], cst.t[0:8, 1:2], cnt8.ap(),
                                start=True, stop=True),
             reads=(cst, cnt8), writes=bkGO)
        glob = sbt(st4, "glob", [1, 16], F32)
        b.op("vector", lambda e: e.tensor_copy(glob.ap(), apGO[:, 0:16]),
             reads=bkGO, writes=(glob,))
        b.dma("scalar", sp_ro[:, :], offs.ap(), reads=(offs,), writes=(spro_r,))
        offl = sbt(st4, "offl", [1, 16 * NTT], F32)
        b.dma("scalar", offl.ap(), sp_ro.rearrange("(o a) b -> o (a b)", o=1),
              reads=(spro_r,), writes=(offl,))
        gvv = glob.t[:, :].unsqueeze(1).broadcast_to([1, NTT, 16])
        b.op("vector", lambda e, gvv=gvv: e.tensor_tensor(
            offl.ap().rearrange("o (a c) -> o a c", c=16),
            offl.ap().rearrange("o (a c) -> o a c", c=16), gvv, ALU.add),
            reads=(offl, glob), writes=(offl,))
        ohi = sbt(st4, "ohi", [128, NE], F32)
        tm8 = sbt(st4, "tm8", [128, NE], F32)
        grs = sbt(st4, "grs", [128, 1], F32)
        eix = sbt(st4, "eix", [128, 1], F32)
        kfl = sbt(st4, "kfl", [128, 1], F32)
        flt = sbt(st4, "flt", [128, 1], F32)
        vvb = [sbt(st4, f"vv{i}", [128, 2], F32) for i in range(2)]
        ixsb = [sbt(st4, f"ixs{i}", [128, 1], I32) for i in range(2)]
        for tt in range(NTT):
            for kk in range(2):
                src = comb if kk == 0 else comb2
                ap2, bk2 = pv(0, 128, 512, 8)
                b.mm(lambda e, ap2=ap2, tt=tt, kk=kk: e.matmul(
                    ap2[:, 0:NE], or128.ap(),
                    bass.AP(offl.t, tt * 16 + kk * 8, [[16 * NTT, 1], [1, 8]]),
                    start=True, stop=True),
                    reads=(or128, offl), writes=bk2)
                b.op("vector", lambda e, ap2=ap2, tt=tt, kk=kk: e.tensor_add(
                    keepm.ap(), Pl.t[:, tt, kk, :], ap2[:, 0:NE]),
                    reads=(Pl,) + bk2, writes=(keepm,))
                b.op("vector", lambda e, tt=tt, src=src: e.tensor_single_scalar(
                    ohi.ap(), src.t[:, tt, :], 0.0, ALU.is_gt),
                    reads=(src,), writes=(ohi,))
                b.op("vector", lambda e: e.tensor_mul(tm8.ap(), keepm.ap(), ohi.ap()),
                     reads=(keepm, ohi), writes=(tm8,))
                b.op("vector", lambda e: e.reduce_sum(grs.ap(), tm8.ap(), AX.X),
                     reads=(tm8,), writes=(grs,))
                b.op("vector", lambda e: e.tensor_single_scalar(
                    keepm.ap(), keepm.ap(), float(CAP), ALU.is_lt),
                    reads=(keepm,), writes=(keepm,))
                b.op("vector", lambda e: e.tensor_mul(tm8.ap(), keepm.ap(), ohi.ap()),
                     reads=(keepm, ohi), writes=(tm8,))
                b.op("vector", lambda e: e.reduce_sum(kfl.ap(), tm8.ap(), AX.X),
                     reads=(tm8,), writes=(kfl,))
                b.op("vector", lambda e: e.tensor_mul(tm8.ap(), erow.ap(), ohi.ap()),
                     reads=(erow, ohi), writes=(tm8,))
                b.op("vector", lambda e: e.reduce_sum(eix.ap(), tm8.ap(), AX.X),
                     reads=(tm8,), writes=(eix,))
                b.op("vector", lambda e: e.tensor_scalar_add(
                    flt.ap(), grs.ap(), float(-CAPP)),
                    reads=(grs,), writes=(flt,))
                b.op("vector", lambda e: e.tensor_mul(flt.ap(), flt.ap(), kfl.ap()),
                     reads=(flt, kfl), writes=(flt,))
                b.op("vector", lambda e: e.tensor_scalar_add(
                    flt.ap(), flt.ap(), float(CAPP)),
                    reads=(flt,), writes=(flt,))
                b.op("vector", lambda e: e.tensor_add(flt.ap(), flt.ap(), eix.ap()),
                     reads=(flt, eix), writes=(flt,))
                b.op("vector", lambda e, tt=tt, kk=kk: e.tensor_copy(
                    idxg.t[:, tt, kk:kk + 1], flt.ap()),
                    reads=(flt,), writes=(idxg,))
                b.op("vector", lambda e: e.tensor_add(flt.ap(), flt.ap(), eix.ap()),
                     reads=(flt, eix), writes=(flt,))
                b.op("vector", lambda e, kk=kk: e.tensor_scalar_add(
                    flt.ap(), flt.ap(), float(kk * CAPQ)),
                    reads=(flt,), writes=(flt,))
                ixs = ixsb[kk]
                b.op("vector", lambda e, ixs=ixs: e.tensor_copy(ixs.ap(), flt.ap()),
                     reads=(flt,), writes=(ixs,))
                vv = vvb[kk]
                b.op("vector", lambda e, tt=tt, vv=vv: e.tensor_copy(
                    vv[:, 0:1], gtb.t[:, tt:tt + 1]), reads=(gtb,), writes=(vv,))
                b.op("vector", lambda e, tt=tt, src=src, vv=vv: e.reduce_sum(
                    vv[:, 1:2], src.t[:, tt, :], AX.X), reads=(src,), writes=(vv,))
                b.op("vector", lambda e, vv=vv: e.tensor_mul(
                    vv[:, 1:2], vv[:, 1:2], kfl.ap()), reads=(vv, kfl), writes=(vv,))
                b.idma(tbl_d, IOA(ixs.ap(), 0), vv.ap(), None,
                       reads=(ixs, vv), writes=(tbl_r,))
        b.cc("ReduceScatter", ALU.add, [[0, 1, 2, 3, 4, 5, 6, 7]],
             tbl_d, tmn_d, reads=(tbl_r,), writes=(tmn_r,))
        b.barrier()
        st4.close()

        if PHASE_LIMIT < 5:
            b.dma("sync", out_d.rearrange("(a p) f -> p a f", p=128), x1.ap(),
                  reads=(x1,), writes=(out_r,))
            b.barrier()
            b.flush()
            return nc
        Hb = sbt(st5, "Hb", [128, 16, 512], BF16)
        xqb = [sbt(st5, f"xq{i}", [128, 8, 512], BF16) for i in range(2)]
        sg = [sbt(st5, "sg0", [128, 512], BF16)] * 2
        xgab = [sbt(st5, "xga0", [128, DM], BF16)] * 2
        ysbb = [sbt(st5, f"ysb{i}", [128, DM], BF16) for i in range(2)]
        grb = [sbt(st5, f"gr{i}", [128, DM], BF16) for i in range(2)]
        tbn = sbt(st5, "tbn", [128, 2, NJK, 2], F32)
        tki = sbt(st5, "tki", [128, 2, NJK], I32)

        for kb in range(2):
            b.dma("gpsimd", tbn.t[:, kb, :, :],
                  tmn_d[kb * CAPQ:kb * CAPQ + CAPP]
                  .rearrange("(j p) c -> p j c", p=128),
                  reads=(tmn_r,), writes=(tbn,))
        zpad = ysbb[0]
        padR = Res("agy_pad")
        b.op("vector", lambda e, zpad=zpad: e.memset(zpad.ap(), 0.0),
             writes=(zpad,))
        for kb in range(2):
            b.dma("gpsimd", agy_in[kb * CAPQ + CAPP:(kb + 1) * CAPQ, :], zpad.ap(),
                  reads=(zpad,), writes=(padR,))
        b.op("vector", lambda e: e.tensor_copy(tki.ap(), tbn.t[:, :, :, 0]),
             reads=(tbn,), writes=(tki,))

        outv = out_d.rearrange("(a p) f -> p a f", p=128)

        def emit_home(tt, kk):
            gr = grb[tt % 2]
            b.idma(gr.ap(), None, agy_o[kk], IOA(idxg.t[:, tt, kk:kk + 1], 0),
                   reads=(agyoR[kk], idxg), writes=(gr,))
            b.op("vector", lambda e, tt=tt, gr=gr: e.tensor_add(
                x1.t[:, tt, :], x1.t[:, tt, :], gr.ap()),
                reads=(gr, x1), writes=(x1,))
            if kk == 1:
                b.dma("sync", outv[:, tt, :], x1.t[:, tt, :],
                      reads=(x1,), writes=(out_r,))

        for kb in range(2):
            for tq in range(NQK):
                gtq = kb * NQK + tq
                for ji in range(4):
                    jj = tq * 4 + ji
                    xga = xgab[jj % 2]
                    b.idma(xga.ap(), None, agxf_out,
                           IOA(tki.t[:, kb, jj:jj + 1], 0),
                           reads=(agxf_r, tki), writes=(xga,))
                    b.dma("gpsimd",
                          xe_d[(gtq * 4 + ji) * 128:(gtq * 4 + ji + 1) * 128, :],
                          xga.ap(), reads=(xga,), writes=(xeR[gtq % 2],))
                if kb == 1 and tq == 0:
                    b.cc("AllGather", ALU.bypass, [[0, 1, 2, 3, 4, 5, 6, 7]],
                         agy_in[0:CAPQ, :], agy_o[0],
                         reads=(agyiR[0], padR), writes=(agyoR[0],))
                xq = xqb[gtq % 2]
                for d in range(8):
                    b.dma("sync", xq.t[:, d, :],
                          xe_d[gtq * 512:(gtq + 1) * 512, d * 128:(d + 1) * 128],
                          reads=(xeR[gtq % 2],), writes=(xq,), transpose=True)
                for dh in range(16):
                    apGm, bkGm = pv(0, 128, 512 * (dh % 2), 512)
                    apUm, bkUm = pv(0, 128, 1024 + 512 * (dh % 2), 512)
                    b.mm([lambda e, k=k, dh=dh, xq=xq, apGm=apGm: e.matmul(
                        apGm[:, 0:512], wg.t[:, k, dh * 128:(dh + 1) * 128],
                        xq.t[:, k, :], start=(k == 0), stop=(k == 7))
                        for k in range(8)],
                        reads=(wg, xq), writes=bkGm)
                    b.mm([lambda e, k=k, dh=dh, xq=xq, apUm=apUm: e.matmul(
                        apUm[:, 0:512], wu.t[:, k, dh * 128:(dh + 1) * 128],
                        xq.t[:, k, :], start=(k == 0), stop=(k == 7))
                        for k in range(8)],
                        reads=(wu, xq), writes=bkUm)
                    sgt = sg[dh % 2]
                    b.op("scalar", lambda e, sgt=sgt, apGm=apGm: e.activation(
                        sgt.ap(), apGm[:, 0:512], AF.Silu),
                        reads=bkGm, writes=(sgt,))
                    b.op("vector", lambda e, dh=dh, sgt=sgt, apUm=apUm: e.tensor_tensor(
                        Hb.t[:, dh, :], sgt.ap(), apUm[:, 0:512], ALU.mult),
                        reads=(sgt,) + bkUm, writes=(Hb,))
                for ji in range(4):
                    apYm, bkYm = pv(0, 128, 1024 * ji, 1024)
                    fns = []
                    for dh in range(16):
                        for n0 in (0, 512):
                            fns.append(lambda e, dh=dh, ji=ji, n0=n0, apYm=apYm:
                                       e.matmul(
                                           apYm[:, n0:n0 + 512],
                                           Hb.t[:, dh, ji * 128:(ji + 1) * 128],
                                           wd.t[:, dh, n0:n0 + 512],
                                           start=(dh == 0), stop=(dh == 15)))
                    b.mm(fns, reads=(Hb, wd), writes=bkYm)
                for ji in range(4):
                    jj = tq * 4 + ji
                    apYm, bkYm = pv(0, 128, 1024 * ji, 1024)
                    ysb = ysbb[ji % 2]
                    b.op("vector", lambda e, kb=kb, jj=jj, apYm=apYm, ysb=ysb:
                         e.tensor_scalar_mul(
                             ysb.ap(), apYm[:, 0:DM], tbn.t[:, kb, jj, 1:2]),
                         reads=bkYm + (tbn,), writes=(ysb,))
                    b.dma("scalar",
                          agy_in[kb * CAPQ + jj * 128:kb * CAPQ + (jj + 1) * 128, :],
                          ysb.ap(), reads=(ysb,), writes=(agyiR[kb],))
        b.cc("AllGather", ALU.bypass, [[0, 1, 2, 3, 4, 5, 6, 7]],
             agy_in[CAPQ:2 * CAPQ, :], agy_o[1],
             reads=(agyiR[1], padR), writes=(agyoR[1],))
        for tt in range(NTT):
            emit_home(tt, 0)
        for tt in range(NTT):
            emit_home(tt, 1)
        st5.close()

        b.barrier()
        b.flush()
    return nc



def kernel(L0_attn_u, L0_attn_W_in, L0_attn_conv_w, L0_attn_conv_b,
           L0_attn_dt_bias, L0_attn_A_log, L0_attn_D, L0_attn_norm_w,
           L0_attn_W_out, L0_ffn_gate_w, L0_ffn_gate_proj_w, L0_ffn_up_proj_w,
           L0_ffn_down_proj_w):
    u = np.asarray(L0_attn_u, dtype=np.float32)
    Bsz, S, _ = u.shape
    assert Bsz == 2
    TLOC = S // 4
    nc = build_program(TLOC)

    WinT = np.ascontiguousarray(np.asarray(L0_attn_W_in, np.float32).T)
    convw = np.ascontiguousarray(
        np.asarray(L0_attn_conv_w, np.float32).reshape(NXBC, 128, 4).transpose(1, 0, 2))
    convb = np.ascontiguousarray(
        np.asarray(L0_attn_conv_b, np.float32).reshape(NXBC, 128).T)
    dtb = np.ascontiguousarray(np.asarray(L0_attn_dt_bias, np.float32).reshape(NH, 1))
    Alog = np.ascontiguousarray(np.asarray(L0_attn_A_log, np.float32).reshape(NH, 1))
    Dhp = np.ascontiguousarray(np.repeat(np.asarray(L0_attn_D, np.float32), HD)[None, :])
    nrmw = np.ascontiguousarray(
        np.asarray(L0_attn_norm_w, np.float32).reshape(16, 128).T)
    WoutT = np.ascontiguousarray(np.asarray(L0_attn_W_out, np.float32).T)
    gateT = np.ascontiguousarray(np.asarray(L0_ffn_gate_w, np.float32).T)
    WgT = np.ascontiguousarray(np.asarray(L0_ffn_gate_proj_w, np.float32).transpose(0, 2, 1))
    WuT = np.ascontiguousarray(np.asarray(L0_ffn_up_proj_w, np.float32).transpose(0, 2, 1))
    WdT = np.ascontiguousarray(np.asarray(L0_ffn_down_proj_w, np.float32).transpose(0, 2, 1))

    CAP = int(CAPF * (8 * TLOC) / NE)
    CAPP = 512 * ((CAP + 511) // 512)
    NSLOTP = 2 * (CAPP + 128)
    SPPP = NSLOTP // 128
    NTT = TLOC // 128
    pp = np.arange(128, dtype=np.float32)

    in_maps = []
    for c in range(8):
        bi, q = divmod(c, 4)
        s0 = q * TLOC
        useg = u[bi, s0:s0 + TLOC]
        gtok = (c * TLOC + 128.0 * np.arange(NTT, dtype=np.float32)[None, :]
                + pp[:, None]).astype(np.float32)
        sidx = np.zeros((128, 26), np.float32)
        sidx[:, 0] = c * NSLOTP + SPPP * pp
        for k in range(8):
            sidx[:, 1 + k] = c * DM + 128 * k + pp
        for dh in range(16):
            sidx[:, 9 + dh] = c * DHID + 128 * dh + pp
        halo = np.zeros((4, DM), np.float32) if q == 0 else u[bi, s0 - 4:s0]
        uTh = np.ascontiguousarray(np.concatenate([halo, useg], 0).T)
        cc = np.zeros((64, 16), np.float32)
        for j in range(8):
            cc[j, 0] = 1.0 if (j // 4 == bi and j < c) else 0.0
            cc[j, 1] = 1.0 if j < c else 0.0
        for k in range(8):
            for j in range(8):
                if j < k < c and k // 4 == bi and j // 4 == bi:
                    cc[k, 2 + j] = 1.0
        in_maps.append({
            "uT": uTh, "u_tok": np.ascontiguousarray(useg), "cconst": cc,
            "WinT": WinT, "convw": convw, "convb": convb, "dtb": dtb,
            "Alog": Alog, "Dhp": Dhp, "nrmw": nrmw, "WoutT": WoutT,
            "gateT": gateT, "WgT": WgT, "WuT": WuT, "WdT": WdT,
            "gtok": gtok, "sidx": sidx,
        })

    res = run_bass_kernel_spmd(nc, in_maps, core_ids=list(range(8)))
    full = np.empty((2, S, DM), np.float32)
    for c in range(8):
        bi, q = divmod(c, 4)
        full[bi, q * TLOC:(q + 1) * TLOC] = res.results[c]["out"]
    return full
